# version 13
# baseline (speedup 1.0000x reference)
import contextlib
import numpy as np
import ml_dtypes
import concourse.bass as bass
import concourse.mybir as mybir
from concourse.bass_utils import run_bass_kernel_spmd

F32 = mybir.dt.float32
BF16 = mybir.dt.bfloat16
AF = mybir.ActivationFunctionType
OP = mybir.AluOpType
AX = mybir.AxisListType

D = 2048
H = 16
DK = 128
CH = 64
TILE = 512
TB = TILE // 128
NIC = D // 128
NT = 4
NP = 4
HG = 4
SCALE = float(DK) ** -0.5
EPS = 1e-5

DBG_NT = None
DBG_NP = None
DBG_TAPS = ()


def build():
    np_tiles = NP if DBG_NP is None else DBG_NP
    nt_tiles = NT if DBG_NT is None else DBG_NT
    nc = bass.Bass("TRN2", target_bir_lowering=False)

    def din(name, shape):
        return nc.dram_tensor(name, shape, F32, kind="ExternalInput").ap()
    xo_d = din("xo", [2048, D])
    xp_d = din("xp", [2048, D])
    w_d = {"q": din("wq", [D, D]), "k": din("wk", [D, D]),
           "v": din("wv", [D, D]), "o": din("wo", [D, D])}
    fm_d = [din("fq1w", [DK, DK]), din("fq2w", [DK, DK]),
            din("fk1w", [DK, DK]), din("fk2w", [DK, DK])]
    fb_d = [din("fq1b", [DK]), din("fq2b", [DK]),
            din("fk1b", [DK]), din("fk2b", [DK])]
    normw_d = din("normw", [DK])
    pw_d = din("pw", [D, D * 4])
    nw_d = din("nw", [D, D * 4])
    exps_d = din("exps", [4]); bexps_d = din("bexps", [4])
    maskw_d = din("maskw", [4])
    fscale_d = din("fscale", [1]); bscale_d = din("bscale", [1])
    pbias_d = din("pbias", [D, 4]); nbias_d = din("nbias", [D, 4])
    out_d = nc.dram_tensor("out", [2048, D], F32, kind="ExternalOutput").ap()

    sc_wq = nc.dram_tensor("sc_wq", [H, NIC * 2, 128, DK], BF16,
                           kind="Internal").ap()
    sc_wk = nc.dram_tensor("sc_wk", [H, NIC * 2, 128, DK], BF16,
                           kind="Internal").ap()
    sc_wv = nc.dram_tensor("sc_wv", [NIC, 128, 2, D], BF16, kind="Internal").ap()
    sc_wo = nc.dram_tensor("sc_wo", [NIC, 128, 2, D], BF16, kind="Internal").ap()
    sc_wf = nc.dram_tensor("sc_wf", [NIC, 128, 2, D], BF16, kind="Internal").ap()
    sc_bias = nc.dram_tensor("sc_bias", [2, D], BF16, kind="Internal").ap()

    id128f_d = nc.inline_tensor(np.eye(128, dtype=np.float32), name="id128f").ap()
    trimask_d = nc.inline_tensor(np.triu(np.ones((128, 128))).astype(
        ml_dtypes.bfloat16), name="trimask").ap()

    taps = {}
    for tname in DBG_TAPS:
        shp = {"lnf": [D, TILE], "q0": [DK, TILE], "kp0": [DK, TILE],
               "state": [DK, DK], "attn": [TILE, D], "x2": [TILE, D],
               "y": [TILE, D]}[tname]
        taps[tname] = nc.dram_tensor("tap_" + tname, shp, F32,
                                     kind="ExternalOutput").ap()

    with contextlib.ExitStack() as stack:
        def sbuf(name, shape, dt=F32):
            return stack.enter_context(nc.sbuf_tensor(name, shape, dt))

        x_t = sbuf("x_t", [128, TB * D])
        wbuf = sbuf("wbuf", [128, TB * D])
        lnf_hi = sbuf("lnf_hi", [128, NIC * TILE], BF16)
        lnf_lo = sbuf("lnf_lo", [128, NIC * TILE], BF16)
        o_hi = sbuf("o_hi", [128, H * TILE], BF16)
        o_lo = sbuf("o_lo", [128, H * TILE], BF16)
        v_hi = sbuf("v_hi", [128, TB * HG * DK], BF16)
        v_lo = sbuf("v_lo", [128, TB * HG * DK], BF16)
        st_hi = sbuf("st_hi", [128, H * DK], BF16)
        st_lo = sbuf("st_lo", [128, H * DK], BF16)
        st_f = sbuf("st_f", [128, H * DK])
        wsq = sbuf("wsq", [128, NIC * 2 * DK], BF16)
        wsk = sbuf("wsk", [128, NIC * 2 * DK], BF16)
        wsv = sbuf("wsv", [128, 4 * 2 * 512], BF16)
        wst = sbuf("wst", [128, 4 * 2 * 512], BF16)
        q_hi = sbuf("q_hi", [128, TILE], BF16); q_lo = sbuf("q_lo", [128, TILE], BF16)
        k_hi = sbuf("k_hi", [128, TILE], BF16); k_lo = sbuf("k_lo", [128, TILE], BF16)
        qp_hi = sbuf("qp_hi", [128, TILE], BF16); qp_lo = sbuf("qp_lo", [128, TILE], BF16)
        kp_hi = sbuf("kp_hi", [128, TILE], BF16); kp_lo = sbuf("kp_lo", [128, TILE], BF16)
        kt_hi = sbuf("kt_hi", [128, TILE], BF16); kt_lo = sbuf("kt_lo", [128, TILE], BF16)
        sm_hi = sbuf("sm_hi", [128, 512], BF16); sm_lo = sbuf("sm_lo", [128, 512], BF16)
        fmw_hi = sbuf("fmw_hi", [128, 4 * DK], BF16)
        fmw_lo = sbuf("fmw_lo", [128, 4 * DK], BF16)
        fmb = sbuf("fmb", [128, 4])
        fbias = sbuf("fbias", [128, 4])
        nw_sb = sbuf("nw_sb", [128, 1])
        mv = sbuf("mv", [128, 24])
        stat = sbuf("stat", [128, 2 * TB])
        rstd4 = sbuf("rstd4", [128, TB])
        gn_s = sbuf("gn_s", [128, 4]); gn_r = sbuf("gn_r", [128, 4])
        cc = sbuf("cc", [128, 4]); ncc = sbuf("ncc", [128, 4])
        bs_c = sbuf("bs_c", [128, 1])
        bias_hl = sbuf("bias_hl", [1, 2 * D], BF16)
        id128f = sbuf("id128f_s", [128, 128])
        trimask = sbuf("trimask_s", [128, 128], BF16)
        ones_bf = sbuf("ones_bf", [1, 128], BF16)
        row4 = sbuf("row4", [1, 4]); row4b = sbuf("row4b", [1, 4])
        bs11 = sbuf("bs11", [1, 1]); fs11 = sbuf("fs11", [1, 1])
        b16hi = sbuf("b16hi", [128, 16], BF16); b16lo = sbuf("b16lo", [128, 16], BF16)

        fmstage = x_t[:, 0:128]
        on_f32 = wbuf[:, 2560:3072]
        b16r = x_t[:, 128:192]
        b16 = x_t[:, 256:272]; b16b = x_t[:, 272:288]
        sgn = x_t[:, 288:304]; sgn2 = x_t[:, 304:320]
        ones_f = x_t[0:1, 320:448]
        b16br = x_t[:, 192:256]
        pb = [stack.enter_context(nc.psum_tensor(f"pb{i}", [128, 512], F32))
              for i in range(8)]

        class Sem:
            def __init__(self, name):
                self.h = stack.enter_context(nc.semaphore(name))
                self.n = 0
            def inc(self, inst, k=1):
                inst.then_inc(self.h, k)
                self.n += k
                return (self, self.n)
            def now(self):
                return (self, self.n)
        s_misc = Sem("s_misc"); s_ld = Sem("s_ld"); s_x = Sem("s_x")
        s_wq = Sem("s_wq"); s_wk = Sem("s_wk")
        s_wv = [Sem("s_wv0"), Sem("s_wv1"), Sem("s_wv2"), Sem("s_wv3")]
        s_wt = [Sem("s_wt0"), Sem("s_wt1"), Sem("s_wt2"), Sem("s_wt3")]
        s_stg = [Sem("s_stg0"), Sem("s_stg1")]
        s_out = Sem("s_out"); s_tap = Sem("s_tap")
        s_ld2 = Sem("s_ld2")
        dma_sems = ([s_misc, s_ld, s_x, s_wq, s_wk, s_out, s_tap, s_ld2,
                     s_stg[0], s_stg[1]] + s_wv + s_wt)
        sT = Sem("sT"); sV = Sem("sV"); sS = Sem("sS")

        def wait(eng, ev):
            if ev is not None:
                eng.wait_ge(ev[0].h, ev[1])

        block = stack.enter_context(nc.Block())

        @block.sync
        def _(sync):
            T, V, S, G, SY = nc.tensor, nc.vector, nc.scalar, nc.gpsimd, nc.sync
            eps_c, scale_c, c23, inv128 = (None,) * 4

            def load(dst, src, sem, ev=None, eng=None):
                e = SY if eng is None else eng
                wait(e, ev)
                return sem.inc(e.dma_start(dst, src), 16)

            def load_ser(dst, src):
                ev = s_misc.inc(SY.dma_start(dst, src), 16)
                wait(SY, ev)
                return ev

            def store(dst, src, sem, ev=None, slow=False):
                wait(G, ev)
                return sem.inc(G.dma_start(dst, src,
                                           allow_slow_non_contiguous=slow), 16)

            def mm3(out, lT_hi, lT_lo, r_hi, r_lo, start, stop, tp=None):
                T.matmul(out, lT_hi, r_hi, start=start, stop=False, tile_position=tp)
                T.matmul(out, lT_hi, r_lo, start=False, stop=False, tile_position=tp)
                return T.matmul(out, lT_lo, r_hi, start=False, stop=stop,
                                tile_position=tp)

            def split(ps, hi, lo, scale_ap=None, wait_s=None, wait_v=None):
                wait(S, wait_s)
                if scale_ap is None:
                    ih = S.copy(hi, ps)
                else:
                    ih = S.activation(hi, ps, AF.Copy, bias=0.0, scale=scale_ap)
                eh = sS.inc(ih)
                wait(V, wait_v)
                wait(V, eh)
                il = V.scalar_tensor_tensor(
                    lo, ps, (1.0 if scale_ap is None else scale_ap), hi,
                    OP.mult, OP.subtract)
                return eh, sV.inc(il)

            def split_sbuf(src_f32, hi, lo, ev_src):
                wait(S, ev_src)
                eh = sS.inc(S.copy(hi, src_f32))
                wait(V, eh)
                il = V.tensor_tensor(lo, src_f32, hi, OP.subtract)
                return eh, sV.inc(V.drain())

            V.memset(st_hi[:], 0.0)
            V.memset(st_lo[:], 0.0)
            V.memset(st_f[:], 0.0)
            V.memset(ones_bf[:], 1.0)
            V.memset(ones_f[:], 1.0)
            ev_ms = sV.inc(V.drain())

            load_ser(id128f[:], id128f_d)
            load_ser(trimask[:], trimask_d)
            load_ser(nw_sb[:], normw_d.rearrange("(p one) -> p one", one=1))
            for j in range(4):
                ev_fbl = load_ser(fbias[:, j:j + 1],
                                  fb_d[j].rearrange("(p one) -> p one", one=1))

            ev_tr_prev = None
            ev_fm_lo = None
            for j in range(4):
                ev_l = load(fmstage[:], fm_d[j], s_ld,
                            ev=(ev_fm_lo if j == 1 else ev_tr_prev))
                wait(T, ev_l)
                wait(T, ev_ms)
                if j > 0:
                    wait(T, ev_fm_lo)
                ev_tr_prev = sT.inc(T.transpose(pb[0][:, 0:128], fmstage[:],
                                                id128f[:]))
                if j == 0:
                    wait(V, ev_tr_prev)
                    V.tensor_scalar(fmstage[:], pb[0][:, 0:128], SCALE,
                                    None, OP.mult)
                    ev_fs = sV.inc(V.drain())
                    _, ev_fm_lo = split_sbuf(fmstage[:],
                                             fmw_hi[:, 0:DK],
                                             fmw_lo[:, 0:DK], ev_fs)
                else:
                    _, ev_fm_lo = split(pb[0][:, 0:128],
                                        fmw_hi[:, j * DK:(j + 1) * DK],
                                        fmw_lo[:, j * DK:(j + 1) * DK],
                                        wait_s=ev_tr_prev)
            psum_gate = ev_fm_lo
            wait(V, ev_fbl)
            V.tensor_scalar(fmb[:, 0:1], fbias[:, 0:1], SCALE, None, OP.mult)
            V.tensor_copy(fmb[:, 1:4], fbias[:, 1:4])
            sV.inc(V.drain())

            def stg_ap(buf):
                return lnf_hi[:, buf * 4096:(buf + 1) * 4096]
            wbuf_gate = [None, None]
            stg_gate = [None, None]

            def prep_weight_chunk(src_f32, which, oc, stg_override=None):
                nonlocal psum_gate
                buf = oc % 2
                stg = stg_ap(buf) if stg_override is None else stg_override
                wait(T, psum_gate)
                evt = []
                e = None
                for bk in range(4):
                    for jj in range(4):
                        ic = bk * 4 + jj
                        e = T.transpose(pb[bk][:, jj * 128:(jj + 1) * 128],
                                        src_f32[:, ic * 128:(ic + 1) * 128],
                                        id128f[:])
                    evt.append(sT.inc(e))
                ev_lo_all = None
                stg4 = stg.rearrange("p (ic hl o) -> p ic hl o", ic=NIC, hl=2)
                wait(S, stg_gate[buf])
                wait(V, stg_gate[buf])
                for bk in range(4):
                    hi_ap = stg4[:, bk * 4:(bk + 1) * 4, 0, :]
                    lo_ap = stg4[:, bk * 4:(bk + 1) * 4, 1, :]
                    sc = nw_sb[:, 0:1] if which == "o" else None
                    _, ev_lo_all = split(pb[bk][:, 0:512], hi_ap, lo_ap,
                                         scale_ap=sc, wait_s=evt[bk])
                psum_gate = ev_lo_all
                if which in ("q", "k"):
                    dsc = sc_wq if which == "q" else sc_wk
                    dst = dsc[oc].rearrange("g p o -> p g o")
                    ev_w = store(dst, stg.rearrange("p (g o) -> p g o", g=2 * NIC),
                                 s_stg[buf], ev=ev_lo_all)
                else:
                    dsc = {"v": sc_wv, "o": sc_wo, "f": sc_wf}[which]
                    dst = dsc[:, :, :, oc * 128:(oc + 1) * 128].rearrange(
                        "ic p hl o -> p ic hl o")
                    store(dst[:, :, 0, :], stg4[:, :, 0, :], s_stg[buf],
                          ev=ev_lo_all)
                    ev_w = store(dst[:, :, 1, :], stg4[:, :, 1, :], s_stg[buf])
                stg_gate[buf] = ev_w
                return (evt[-1])

            tr_ev = [ev_tr_prev, ev_tr_prev]
            ldsem = [s_ld, s_x]
            gidx = 0
            for which in ("q", "k", "v", "o"):
                wd = w_d[which]
                for oc in range(NIC):
                    buf = gidx % 2
                    region = wbuf[:, buf * D:(buf + 1) * D]
                    ev_l = load(region, wd[oc * 128:(oc + 1) * 128, :],
                                ldsem[buf], ev=tr_ev[buf])
                    wait(T, ev_l)
                    tr_ev[buf] = prep_weight_chunk(region, which, oc)
                    gidx += 1

            ev1 = load_ser(row4[:], maskw_d.rearrange("(a b) -> a b", a=1))
            ev2 = load_ser(row4b[:], exps_d.rearrange("(a b) -> a b", a=1))
            ev3 = load_ser(fs11[:], fscale_d.rearrange("(a b) -> a b", a=1))
            wait(S, ev1)
            ev_sg1 = sS.inc(S.activation(row4[:], row4[:], AF.Sigmoid,
                                         bias=0.0, scale=1.0))
            wait(V, ev_sg1); wait(V, ev2); wait(V, ev3)
            V.tensor_tensor(row4[:], row4[:], row4b[:], OP.mult)
            V.drain()
            V.tensor_scalar(row4[:], row4[:], fs11[0:1, 0:1], None, OP.mult)
            ev_c = sV.inc(V.drain())
            wait(T, ev_c)
            ev_cb = sT.inc(T.matmul(pb[4][:, 0:4], ones_f[:], row4[:],
                                    start=True, stop=True))
            wait(V, ev_cb)
            V.tensor_copy(cc[:], pb[4][:, 0:4])
            V.drain()
            V.tensor_scalar(ncc[:], cc[:], -1.0, None, OP.mult)
            ev_cc = sV.inc(V.drain())

            ffn_slot_gate = [(sT, 0), (sT, 0)]

            def ffn_oc(oc, ev_lnf_tile):
                pacc = x_t[:, 0:D]
                rows = slice(oc * 128, (oc + 1) * 128)
                slots = (wbuf[:, 4096:6144], wbuf[:, 6144:8192])
                qsem = (s_ld, s_ld2)
                for side, (wd_, co) in enumerate(((pw_d, cc), (nw_d, ncc))):
                    for qtr in range(4):
                        si = qtr % 2
                        slot = slots[si]
                        ev_l = load(slot, wd_[rows, qtr * 2048:(qtr + 1) * 2048],
                                    qsem[si], ev=ffn_slot_gate[si])
                        wait(S, ev_l)
                        wait(S, ev_lnf_tile)
                        if oc == 0 and side == 0 and qtr == 0:
                            wait(V, ev_cc)
                        ev_sg = sS.inc(S.activation(slot, slot, AF.Sigmoid,
                                                    bias=0.0, scale=1.0))
                        sp4 = slot.rearrange("p (i n) -> p n i", n=4)
                        dst = pacc[:, qtr * 512:(qtr + 1) * 512]
                        wait(V, ev_sg)
                        if side == 0:
                            V.tensor_scalar(dst, sp4[:, 0, :], co[:, 0:1],
                                            None, OP.mult)
                            V.drain()
                            nstart = 1
                        else:
                            nstart = 0
                        for n in range(nstart, 4):
                            V.scalar_tensor_tensor(dst, sp4[:, n, :],
                                                   co[:, n:n + 1], dst,
                                                   OP.mult, OP.add)
                            V.drain()
                        ffn_slot_gate[si] = sV.inc(V.drain())
                wait(T, ffn_slot_gate[0]); wait(T, ffn_slot_gate[1])
                wait(T, sS.now()); wait(T, sV.now())
                buf = oc % 2
                stg = o_hi[:, buf * 4096:(buf + 1) * 4096]
                ev_tr = prep_weight_chunk(pacc, "f", oc, stg_override=stg)
                st8["x_free"] = ev_tr
                st8["wbuf_free"] = ev_tr
            ev_pb_ = load_ser(b16r.rearrange("p (ic n) -> p ic n", n=4),
                              pbias_d.rearrange("(ic p) n -> p ic n", p=128))
            ev_nb_ = load_ser(b16br.rearrange("p (ic n) -> p ic n", n=4),
                              nbias_d.rearrange("(ic p) n -> p ic n", p=128))
            wait(SY, ev_cb)
            ev_be = load_ser(row4[:], bexps_d.rearrange("(a b) -> a b", a=1))
            wait(V, ev_pb_); wait(V, ev_nb_)
            V.tensor_tensor(b16br[:], b16r[:], b16br[:], OP.subtract)
            ev_d_ = sV.inc(V.drain())
            wait(T, ev_be)
            ev_bb = sT.inc(T.matmul(pb[5][:, 0:4], ones_f[:], row4[:],
                                    start=True, stop=True))
            wait(V, ev_bb)
            V.tensor_copy(fbias[:], pb[5][:, 0:4])
            V.drain()
            d4 = b16br.rearrange("p (ic n) -> p n ic", n=4)
            V.tensor_scalar(b16[:], d4[:, 0, :], fbias[:, 0:1], None, OP.mult)
            V.drain()
            for n in range(1, 4):
                V.scalar_tensor_tensor(b16[:], d4[:, n, :], fbias[:, n:n + 1],
                                       b16[:], OP.mult, OP.add)
                V.drain()
            V.tensor_scalar(b16[:], b16[:], 1.0, None, OP.min)
            V.drain()
            V.tensor_scalar(b16[:], b16[:], -1.0, None, OP.max)
            ev_clip = sV.inc(V.drain())
            wait(S, ev_clip)
            ev_sgn = sS.inc(S.activation(sgn[:], b16[:], AF.Sign, bias=0.0,
                                         scale=1.0))
            ev_ab = sS.inc(S.activation(b16b[:], b16[:], AF.Abs, bias=0.0,
                                        scale=15.0))
            wait(V, ev_ab)
            V.tensor_scalar(sgn2[:], b16b[:], 0.5, None, OP.is_ge)
            V.drain()
            for kk in range(2, 16):
                V.scalar_tensor_tensor(sgn2[:], b16b[:], kk - 0.5, sgn2[:],
                                       OP.is_ge, OP.add)
                V.drain()
            V.tensor_copy(b16b[:], sgn2[:])
            V.drain()
            wait(V, ev_sgn)
            V.tensor_tensor(b16b[:], b16b[:], sgn[:], OP.mult)
            ev_bs0 = sV.inc(V.drain())
            ev_bsl = load_ser(bs11[:], bscale_d.rearrange("(a b) -> a b", a=1))
            wait(T, ev_bsl)
            ev_bsb = sT.inc(T.matmul(pb[6][:, 0:1], ones_f[:], bs11[:],
                                     start=True, stop=True))
            wait(V, ev_bsb)
            V.tensor_copy(bs_c[:], pb[6][:, 0:1])
            V.drain()
            V.tensor_scalar(bs_c[:], bs_c[:], 1.0 / 15.0, None, OP.mult)
            V.drain()
            wait(V, ev_bs0)
            V.tensor_scalar(b16b[:], b16b[:], bs_c[:, 0:1], None, OP.mult)
            V.drain()
            V.tensor_copy(b16hi[:], b16b[:])
            V.drain()
            V.tensor_tensor(b16lo[:], b16b[:], b16hi[:], OP.subtract)
            ev_bv = sV.inc(V.drain())
            ev_bw0 = store(sc_bias[0].rearrange("(ic p) -> p ic", p=128),
                           b16hi[:], s_out, ev=ev_bv, slow=True)
            wait(G, ev_bw0)
            ev_bw = store(sc_bias[1].rearrange("(ic p) -> p ic", p=128), b16lo[:],
                          s_out, slow=True)
            wait(SY, ev_bw)
            ev_bload = load_ser(bias_hl[:],
                                sc_bias.rearrange("a b -> (a b)").rearrange(
                                    "(one ab) -> one ab", one=1))

            st8 = {
                "x_free": None, "wbuf_free": ev_bload, "lnf_free": None,
                "v_free": None, "wsv_gate": [None] * 4, "wst_gate": [None] * 4,
                "stage_q_free": None, "stage_k_free": None,
                "st_ev": [ev_ms] * H,
            }
            ln_psum_gate = [None, None]

            def emit_ln(src):
                for tb in range(TB):
                    blk = src[:, tb * D:(tb + 1) * D]
                    for j in range(4):
                        V.bn_stats(mv[:, j * 6:(j + 1) * 6],
                                   blk[:, j * 512:(j + 1) * 512])
                    V.drain()
                    V.bn_aggr(stat[:, tb * 2:tb * 2 + 2], mv[:])
                    V.drain()
                    V.tensor_scalar(rstd4[:, tb:tb + 1],
                                    stat[:, tb * 2 + 1:tb * 2 + 2],
                                    EPS, None, OP.add)
                ev_stats = sV.inc(V.drain())
                wait(S, ev_stats)
                ev_sq = sS.inc(S.activation(rstd4[:], rstd4[:], AF.Sqrt,
                                            bias=0.0, scale=1.0))
                wait(V, ev_sq)
                V.reciprocal(rstd4[:], rstd4[:])
                V.drain()
                wait(V, st8["wbuf_free"])
                ev_ap = [None] * TB
                for tb in range(TB):
                    V.tensor_scalar(wbuf[:, tb * D:(tb + 1) * D],
                                    src[:, tb * D:(tb + 1) * D],
                                    stat[:, tb * 2:tb * 2 + 1],
                                    rstd4[:, tb:tb + 1], OP.subtract, OP.mult)
                    ev_ap[tb] = sV.inc(V.drain())
                wait(T, st8["lnf_free"])
                wait(T, sS.now()); wait(T, sV.now())
                ev_lnf = None
                for tb in range(TB):
                    wait(T, ev_ap[tb])
                    for grp in range(4):
                        bank = pb[4 + (grp % 2)]
                        wait(T, ln_psum_gate[grp % 2])
                        e = None
                        for j in range(4):
                            ic = grp * 4 + j
                            e = T.transpose(bank[:, j * 128:(j + 1) * 128],
                                            wbuf[:, tb * D + ic * 128:
                                                 tb * D + (ic + 1) * 128],
                                            id128f[:])
                        evt = sT.inc(e)
                        hi4 = lnf_hi.rearrange("p (ic tt) -> p ic tt", ic=NIC)[
                            :, grp * 4:(grp + 1) * 4, tb * 128:(tb + 1) * 128]
                        lo4 = lnf_lo.rearrange("p (ic tt) -> p ic tt", ic=NIC)[
                            :, grp * 4:(grp + 1) * 4, tb * 128:(tb + 1) * 128]
                        wait(S, evt)
                        eh = sS.inc(S.copy(hi4, bank[:, 0:512].rearrange(
                            "p (j c) -> p j c", j=4)))
                        wait(V, eh)
                        il = V.scalar_tensor_tensor(
                            lo4, bank[:, 0:512].rearrange("p (j c) -> p j c",
                                                          j=4),
                            1.0, hi4, OP.mult, OP.subtract)
                        ev_lnf = sV.inc(il)
                        ln_psum_gate[grp % 2] = ev_lnf
                return ev_lnf

            hs = [dict() for _ in range(H)]
            pipe = {"pending_o": None, "last_st": None, "last_kt": None,
                    "last_qpread": None, "last_evac": None, "last_osplit": None,
                    "last_vread": None, "last_ot": None}

            def emit_vproj(g, ev_lnf):
                def wv_load(ic, gate):
                    buf = ic % 4
                    dst = wsv[:, buf * 1024:(buf + 1) * 1024]
                    return load(dst.rearrange("p (hl o) -> p hl o", hl=2),
                                sc_wv[ic, :, :, g * 512:(g + 1) * 512],
                                s_wv[buf], ev=gate)
                wv_ev = [None] * NIC
                for jj in range(4):
                    wv_ev[jj] = wv_load(jj, st8["wsv_gate"][jj])
                wait(T, ev_lnf)
                wait(T, pipe["last_evac"])
                ev_last = None
                for half in range(2):
                    banks = (pb[4], pb[7])
                    if half == 1:
                        wait(T, ev_last)
                    last = [None, None]
                    for ic in range(NIC):
                        buf = ic % 4
                        wait(T, wv_ev[ic])
                        r_hi = wsv[:, buf * 1024:buf * 1024 + 512]
                        r_lo = wsv[:, buf * 1024 + 512:(buf + 1) * 1024]
                        for j in range(2):
                            tb = half * 2 + j
                            lt_hi = lnf_hi[:, ic * TILE + tb * 128:
                                           ic * TILE + (tb + 1) * 128]
                            lt_lo = lnf_lo[:, ic * TILE + tb * 128:
                                           ic * TILE + (tb + 1) * 128]
                            last[j] = mm3(banks[j][:, 0:512], lt_hi, lt_lo,
                                          r_hi, r_lo, start=(ic == 0),
                                          stop=(ic == NIC - 1))
                        ev_ic = sT.inc(last[1])
                        if half == 0:
                            wv_ev[(ic + 4) % NIC] = wv_load((ic + 4) % NIC,
                                                            ev_ic)
                        elif ic + 4 < NIC:
                            wv_ev[ic + 4] = wv_load(ic + 4, ev_ic)
                        if half == 1 and ic >= NIC - 4:
                            st8["wsv_gate"][buf] = ev_ic
                    ev_mm = (sT, sT.n)
                    if half == 0:
                        wait(S, pipe["last_vread"])
                        wait(V, pipe["last_vread"])
                    for j in range(2):
                        tb = half * 2 + j
                        _, ev_last = split(banks[j][:, 0:512],
                                           v_hi[:, tb * 512:(tb + 1) * 512],
                                           v_lo[:, tb * 512:(tb + 1) * 512],
                                           wait_s=ev_mm)
                return ev_last

            def bankpair(h):
                return (pb[0], pb[1]) if h % 2 == 0 else (pb[2], pb[3])

            def proj_mms(ws, bank, ev_w, ev_lnf):
                wait(T, ev_w)
                wait(T, ev_lnf)
                last = None
                for ic in range(NIC):
                    lT_hi = ws[:, ic * 256:ic * 256 + 128]
                    lT_lo = ws[:, ic * 256 + 128:(ic + 1) * 256]
                    r_hi = lnf_hi[:, ic * TILE:(ic + 1) * TILE]
                    r_lo = lnf_lo[:, ic * TILE:(ic + 1) * TILE]
                    last = mm3(bank[:, 0:512], lT_hi, lT_lo, r_hi, r_lo,
                               start=(ic == 0), stop=(ic == NIC - 1))
                return sT.inc(last)

            def emit_proj(h, ev_lnf, phase2):
                bk, bq = bankpair(h)
                if phase2:
                    wsk_h, k_sem, k_gate_key = wsk, s_wk, "stage_k_free"
                elif h % 2 == 0:
                    wsk_h, k_sem, k_gate_key = wsk, s_wk, "stage_k_free"
                else:
                    wsk_h, k_sem, k_gate_key = wsq, s_wq, "stage_q_free"
                ev_wk = load(wsk_h.rearrange("p (g o) -> p g o", g=2 * NIC),
                             sc_wk[h].rearrange("g p o -> p g o"),
                             k_sem, ev=st8[k_gate_key])
                if phase2:
                    ev_wq = load(wsq.rearrange("p (g o) -> p g o", g=2 * NIC),
                                 sc_wq[h].rearrange("g p o -> p g o"),
                                 s_wq, ev=st8["stage_q_free"])
                ev_kp = proj_mms(wsk_h, bk, ev_wk, ev_lnf)
                st8[k_gate_key] = ev_kp
                if phase2:
                    ev_qp = proj_mms(wsq, bq, ev_wq, ev_lnf)
                    st8["stage_q_free"] = ev_qp
                _, ev_kl = split(bk[:, 0:512], k_hi[:], k_lo[:], wait_s=ev_kp)
                hs[h]["kl"] = ev_kl
                if phase2:
                    _, ev_ql = split(bq[:, 0:512], q_hi[:], q_lo[:],
                                     wait_s=ev_qp)
                    hs[h]["ql"] = ev_ql

            def fmap(idx1, idx2, src_hi, src_lo, dst_f32, bank1, bank2,
                     read_gate):
                f1 = mm3(bank1[:, 0:512], fmw_hi[:, idx1 * DK:(idx1 + 1) * DK],
                         fmw_lo[:, idx1 * DK:(idx1 + 1) * DK],
                         src_hi, src_lo, True, True)
                f2 = mm3(bank2[:, 0:512], fmw_hi[:, idx2 * DK:(idx2 + 1) * DK],
                         fmw_lo[:, idx2 * DK:(idx2 + 1) * DK],
                         src_hi, src_lo, True, True)
                ev_f = sT.inc(f2)
                wait(S, ev_f)
                wait(S, last_fmap_mult[0])
                f1b = wbuf[:, 0:512]
                ev_s1 = sS.inc(S.activation(f1b, bank1[:, 0:512], AF.Identity,
                                            bias=fmb[:, idx1:idx1 + 1],
                                            scale=1.0))
                wait(V, ev_s1)
                wait(V, read_gate)
                V.scalar_tensor_tensor(dst_f32, bank2[:, 0:512],
                                       fmb[:, idx2:idx2 + 1], f1b,
                                       OP.add, OP.mult)
                ev = sV.inc(V.drain())
                last_fmap_mult[0] = ev
                return ev

            last_fmap_mult = [None]

            def emit_fmap(h, phase2, tap):
                bk, bq = bankpair(h)
                kp_f32 = wbuf[:, 1536:2048]
                wait(T, hs[h]["kl"])
                if phase2:
                    wait(T, hs[h]["ql"])
                ev_kpf = fmap(2, 3, k_hi[:], k_lo[:], kp_f32, bk, bq,
                              pipe["last_kt"])
                if phase2:
                    _, ev_kpf = split_sbuf2(kp_f32, kp_hi[:], kp_lo[:],
                                            ev_kpf, pipe["last_qpread"])
                hs[h]["kpf"] = ev_kpf
                if phase2:
                    qp_f32 = wbuf[:, 1024:1536]
                    wait(T, hs[h]["ql"])
                    wait(T, pipe["last_evac"])
                    ev_qpf = fmap(0, 1, q_hi[:], q_lo[:], qp_f32, pb[4], pb[7],
                                  None)
                    _, ev_qsp = split_sbuf2(qp_f32, qp_hi[:], qp_lo[:],
                                            ev_qpf, pipe["last_qpread"])
                    hs[h]["qsp"] = ev_qsp
                    if tap and h == 0 and "q0" in taps:
                        wait(V, ev_qsp)
                        V.tensor_tensor(wbuf[:, 3072:3584], qp_hi[:], qp_lo[:],
                                        OP.add)
                        ev = sV.inc(V.drain())
                        store(taps["q0"], wbuf[:, 3072:3584], s_tap, ev=ev)
                    if tap and h == 0 and "kp0" in taps:
                        wait(V, ev_qsp)
                        wait(V, s_tap.now())
                        V.tensor_tensor(wbuf[:, 3072:3584], kp_hi[:], kp_lo[:],
                                        OP.add)
                        ev = sV.inc(V.drain())
                        store(taps["kp0"], wbuf[:, 3072:3584], s_tap, ev=ev)

            def split_sbuf2(src_f32, hi, lo, ev_src, gate):
                wait(S, ev_src)
                wait(S, gate)
                eh = sS.inc(S.copy(hi, src_f32))
                wait(V, eh)
                wait(V, gate)
                V.tensor_tensor(lo, src_f32, hi, OP.subtract)
                return eh, sV.inc(V.drain())

            def state_update(h, ev_kv):
                sl = slice(h * DK, (h + 1) * DK)
                wait(V, ev_kv)
                V.tensor_tensor(st_f[:, sl], st_f[:, sl], pb[5][:, 0:128],
                                OP.add)
                ev_a = sV.inc(V.drain())
                wait(S, ev_a)
                evh = sS.inc(S.copy(st_hi[:, sl], st_f[:, sl]))
                wait(V, evh)
                V.tensor_tensor(st_lo[:, sl], st_f[:, sl], st_hi[:, sl],
                                OP.subtract)
                return sV.inc(V.drain())

            def finish_o(tap):
                hp = pipe["pending_o"]
                if hp is None:
                    return
                pipe["pending_o"] = None
                wait(T, hs[hp]["on"])
                e = None
                for cb in range(TB):
                    e = T.transpose(pb[6][:, cb * 128:(cb + 1) * 128],
                                    on_f32[:, cb * 128:(cb + 1) * 128],
                                    id128f[:])
                ev_ot = sT.inc(e)
                pipe["last_ot"] = ev_ot
                _, ev_os = split(pb[6][:, 0:512],
                                 o_hi[:, hp * TILE:(hp + 1) * TILE],
                                 o_lo[:, hp * TILE:(hp + 1) * TILE],
                                 wait_s=ev_ot)
                pipe["last_osplit"] = ev_os

            def emit_attn(h, hh, ev_v, phase2, tap):
                if phase2:
                    finish_o(tap)
                kp_f32 = wbuf[:, 1536:2048]
                wait(T, hs[h]["kpf"])
                wait(T, pipe["last_st"])
                e = None
                for tb in range(TB):
                    e = T.transpose(pb[5][:, tb * 128:(tb + 1) * 128],
                                    kp_f32[:, tb * 128:(tb + 1) * 128],
                                    id128f[:])
                ev_t = sT.inc(e)
                pipe["last_kt"] = ev_t
                _, ev_ktl = split(pb[5][:, 0:512], kt_hi[:], kt_lo[:],
                                  wait_s=ev_t)

                if not phase2:
                    wait(T, ev_ktl)
                    wait(T, ev_v)
                    wait(T, st8["st_ev"][h])
                    last = None
                    for tb in range(TB):
                        vcol = tb * 512 + hh * 128
                        last = mm3(pb[5][:, 0:128],
                                   kt_hi[:, tb * 128:(tb + 1) * 128],
                                   kt_lo[:, tb * 128:(tb + 1) * 128],
                                   v_hi[:, vcol:vcol + 128],
                                   v_lo[:, vcol:vcol + 128],
                                   start=(tb == 0), stop=(tb == TB - 1))
                    ev_kv = sT.inc(last)
                    st8["st_ev"][h] = state_update(h, ev_kv)
                    pipe["last_st"] = st8["st_ev"][h]
                    return

                smst = wbuf[:, 2048:2560]
                ev_evac = [None] * TB
                ev_sc = [None] * TB

                def emit_scores(cb):
                    bank = pb[4] if cb % 2 == 0 else pb[7]
                    lastm = None
                    for mb in range(cb + 1):
                        lastm = mm3(bank[:, mb * 128:(mb + 1) * 128],
                                    kp_hi[:, mb * 128:(mb + 1) * 128],
                                    kp_lo[:, mb * 128:(mb + 1) * 128],
                                    qp_hi[:, cb * 128:(cb + 1) * 128],
                                    qp_lo[:, cb * 128:(cb + 1) * 128],
                                    start=(mb == 0), stop=(mb == cb))
                    ev_sc[cb] = sT.inc(lastm)

                def emit_evac(cb, ev_o_prev=None):
                    bank = pb[4] if cb % 2 == 0 else pb[7]
                    wait(S, ev_o_prev)
                    wait(V, ev_o_prev)
                    if cb > 0:
                        wait(S, ev_sc[cb])
                        eh = sS.inc(S.copy(sm_hi[:, 0:cb * 128],
                                           bank[:, 0:cb * 128]))
                        wait(V, eh)
                        V.scalar_tensor_tensor(sm_lo[:, 0:cb * 128],
                                               bank[:, 0:cb * 128], 1.0,
                                               sm_hi[:, 0:cb * 128],
                                               OP.mult, OP.subtract)
                        V.drain()
                    wait(V, ev_sc[cb])
                    V.tensor_tensor(smst[0:128, 0:128],
                                    bank[:, cb * 128:(cb + 1) * 128],
                                    trimask[:], OP.mult)
                    V.drain()
                    V.tensor_copy(sm_hi[:, cb * 128:(cb + 1) * 128],
                                  smst[0:128, 0:128])
                    V.drain()
                    V.tensor_tensor(sm_lo[:, cb * 128:(cb + 1) * 128],
                                    smst[0:128, 0:128],
                                    sm_hi[:, cb * 128:(cb + 1) * 128],
                                    OP.subtract)
                    ev_evac[cb] = sV.inc(V.drain())

                def emit_o(cb):
                    wait(T, ev_evac[cb])
                    if cb == 0:
                        wait(T, pipe["last_osplit"])
                        wait(T, ev_v)
                        wait(T, st8["st_ev"][h])
                    ocol = cb * 128
                    mm3(pb[6][:, ocol:ocol + 128],
                        qp_hi[:, cb * 128:(cb + 1) * 128],
                        qp_lo[:, cb * 128:(cb + 1) * 128],
                        st_hi[:, h * DK:(h + 1) * DK],
                        st_lo[:, h * DK:(h + 1) * DK],
                        start=(cb == 0), stop=False)
                    last = None
                    for mb in range(cb + 1):
                        vcol = mb * 512 + hh * 128
                        last = mm3(pb[6][:, ocol:ocol + 128],
                                   sm_hi[:, mb * 128:(mb + 1) * 128],
                                   sm_lo[:, mb * 128:(mb + 1) * 128],
                                   v_hi[:, vcol:vcol + 128],
                                   v_lo[:, vcol:vcol + 128],
                                   start=False, stop=(cb == TB - 1 and
                                                      mb == cb))
                    return sT.inc(last)

                wait(T, hs[h]["qsp"])
                emit_scores(0)
                emit_evac(0)
                emit_scores(1)
                ev_o = None
                for cb in range(TB):
                    ev_o = emit_o(cb)
                    if cb + 1 <= TB - 1:
                        emit_evac(cb + 1, ev_o)
                    if cb + 2 <= TB - 1:
                        emit_scores(cb + 2)
                pipe["last_qpread"] = ev_o
                pipe["last_evac"] = ev_evac[TB - 1]
                wait(T, ev_ktl)
                last = None
                for tb in range(TB):
                    vcol = tb * 512 + hh * 128
                    last = mm3(pb[5][:, 0:128],
                               kt_hi[:, tb * 128:(tb + 1) * 128],
                               kt_lo[:, tb * 128:(tb + 1) * 128],
                               v_hi[:, vcol:vcol + 128],
                               v_lo[:, vcol:vcol + 128],
                               start=(tb == 0), stop=(tb == TB - 1))
                ev_kv = sT.inc(last)
                pipe["last_vread"] = ev_kv
                st8["st_ev"][h] = state_update(h, ev_kv)
                pipe["last_st"] = st8["st_ev"][h]
                if tap and h == 0 and "state" in taps:
                    wait(V, st8["st_ev"][h])
                    wait(V, s_tap.now())
                    V.tensor_tensor(wbuf[:, 3072:3200], st_hi[:, 0:DK],
                                    st_lo[:, 0:DK], OP.add)
                    ev = sV.inc(V.drain())
                    store(taps["state"], wbuf[:, 3072:3200], s_tap, ev=ev)

                wait(S, ev_kv)
                ev_sq2 = sS.inc(S.activation(on_f32[:], pb[6][:, 0:512],
                                             AF.Square, bias=0.0, scale=1.0))
                wait(V, ev_sq2)
                wait(V, pipe["last_ot"])
                V.tensor_reduce(gn_s[:],
                                on_f32.rearrange("p (c e) -> p c e", e=128),
                                AX.X, OP.add)
                V.drain()
                V.tensor_scalar(gn_r[:], gn_s[:], 1.0 / DK, EPS,
                                OP.mult, OP.add)
                ev_rd2 = sV.inc(V.drain())
                wait(S, ev_rd2)
                ev_sq3 = sS.inc(S.activation(gn_r[:], gn_r[:], AF.Sqrt,
                                             bias=0.0, scale=1.0))
                wait(V, ev_sq3)
                V.reciprocal(gn_r[:], gn_r[:])
                V.drain()
                for cb in range(TB):
                    V.tensor_scalar(on_f32[:, cb * 128:(cb + 1) * 128],
                                    pb[6][:, cb * 128:(cb + 1) * 128],
                                    gn_r[:, cb:cb + 1], None, OP.mult)
                hs[h]["on"] = sV.inc(V.drain())
                pipe["pending_o"] = h

            def emit_wo_ln2(t, tap):
                wait(T, sV.now()); wait(T, sS.now())
                if taps:
                    wait(S, s_tap.now()); wait(V, s_tap.now())
                for fb in range(4):
                    if fb > 0:
                        wait(T, sS.now())
                    for h in range(H):
                        buf = (fb * H + h) % 4
                        dst = wst[:, buf * 1024:(buf + 1) * 1024]
                        e = load(dst.rearrange("p (hl o) -> p hl o", hl=2),
                                 sc_wo[h, :, :, fb * 512:(fb + 1) * 512],
                                 s_wt[buf], ev=st8["wst_gate"][buf])
                        wait(T, e)
                        r_hi = dst[:, 0:512]; r_lo = dst[:, 512:1024]
                        last = None
                        for tb in range(TB):
                            lT_hi = o_hi[:, h * TILE + tb * 128:
                                         h * TILE + (tb + 1) * 128]
                            lT_lo = o_lo[:, h * TILE + tb * 128:
                                         h * TILE + (tb + 1) * 128]
                            last = mm3(pb[tb][:, 0:512], lT_hi, lT_lo,
                                       r_hi, r_lo, start=(h == 0),
                                       stop=(h == H - 1))
                        st8["wst_gate"][buf] = sT.inc(last)
                    ev_mm = (sT, sT.n)
                    wait(S, ev_mm)
                    for tb in range(TB):
                        sS.inc(S.copy(
                            wbuf[:, tb * D + fb * 512:tb * D + (fb + 1) * 512],
                            pb[tb][:, 0:512]))
                ev_ao = sS.now()
                wait(V, ev_ao)
                if tap and "attn" in taps:
                    ev = store(taps["attn"].rearrange("(tb p) i -> p tb i", p=128),
                               wbuf.rearrange("p (tb i) -> p tb i", tb=TB),
                               s_tap, ev=ev_ao)
                    wait(V, ev)
                for tb in range(TB):
                    blk = wbuf[:, tb * D:(tb + 1) * D]
                    for j in range(4):
                        V.bn_stats(mv[:, j * 6:(j + 1) * 6],
                                   blk[:, j * 512:(j + 1) * 512])
                    V.drain()
                    V.bn_aggr(stat[:, tb * 2:tb * 2 + 2], mv[:])
                    V.drain()
                for tb in range(TB):
                    V.tensor_scalar(rstd4[:, tb:tb + 1],
                                    stat[:, tb * 2 + 1:tb * 2 + 2],
                                    EPS, None, OP.add)
                ev_stats = sV.inc(V.drain())
                wait(S, ev_stats)
                ev_sq = sS.inc(S.activation(rstd4[:], rstd4[:], AF.Sqrt,
                                            bias=0.0, scale=1.0))
                wait(V, ev_sq)
                V.reciprocal(rstd4[:], rstd4[:])
                V.drain()
                for tb in range(TB):
                    blk = wbuf[:, tb * D:(tb + 1) * D]
                    V.tensor_scalar(blk, blk, stat[:, tb * 2:tb * 2 + 1],
                                    rstd4[:, tb:tb + 1], OP.subtract, OP.mult)
                    V.drain()
                    V.tensor_tensor(x_t[:, tb * D:(tb + 1) * D],
                                    x_t[:, tb * D:(tb + 1) * D], blk, OP.add)
                    V.drain()
                ev_x2 = sV.inc(V.drain())
                st8["wbuf_free"] = ev_x2
                if tap and "x2" in taps:
                    store(taps["x2"].rearrange("(tb p) i -> p tb i", p=128),
                          x_t.rearrange("p (tb i) -> p tb i", tb=TB), s_tap,
                          ev=ev_x2)

            def emit_ffn(t, ev_lnf3):
                wait(T, st8["wbuf_free"])
                wait(T, ev_lnf3)
                for fb in range(4):
                    if fb > 0:
                        wait(T, sS.now())
                    for tb in range(TB):
                        T.matmul(pb[tb][:, 0:512], ones_bf[:],
                                 bias_hl[0:1, fb * 512:(fb + 1) * 512],
                                 start=True, stop=False)
                        T.matmul(pb[tb][:, 0:512], ones_bf[:],
                                 bias_hl[0:1, D + fb * 512:D + (fb + 1) * 512],
                                 start=False, stop=False)
                    for ic in range(NIC):
                        buf = (fb * NIC + ic) % 2
                        dst = wst[:, buf * 1024:(buf + 1) * 1024]
                        e = load(dst.rearrange("p (hl o) -> p hl o", hl=2),
                                 sc_wf[ic, :, :, fb * 512:(fb + 1) * 512],
                                 s_wt[buf], ev=st8["wst_gate"][buf])
                        wait(T, e)
                        r_hi = dst[:, 0:512]; r_lo = dst[:, 512:1024]
                        last = None
                        for tb in range(TB):
                            lT_hi = lnf_hi[:, ic * TILE + tb * 128:
                                           ic * TILE + (tb + 1) * 128]
                            lT_lo = lnf_lo[:, ic * TILE + tb * 128:
                                           ic * TILE + (tb + 1) * 128]
                            last = mm3(pb[tb][:, 0:512], lT_hi, lT_lo,
                                       r_hi, r_lo, start=False,
                                       stop=(ic == NIC - 1))
                        st8["wst_gate"][buf] = sT.inc(last)
                    ev_mm = (sT, sT.n)
                    wait(S, ev_mm)
                    for tb in range(TB):
                        sS.inc(S.copy(
                            wbuf[:, tb * D + fb * 512:tb * D + (fb + 1) * 512],
                            pb[tb][:, 0:512]))

            def emit_ln4_store(t, tap):
                ev_y = sS.now()
                wait(V, ev_y)
                if tap and "y" in taps:
                    ev = store(taps["y"].rearrange("(tb p) i -> p tb i", p=128),
                               wbuf.rearrange("p (tb i) -> p tb i", tb=TB),
                               s_tap, ev=ev_y)
                    wait(V, ev)
                for tb in range(TB):
                    blk = wbuf[:, tb * D:(tb + 1) * D]
                    for j in range(4):
                        V.bn_stats(mv[:, j * 6:(j + 1) * 6],
                                   blk[:, j * 512:(j + 1) * 512])
                    V.drain()
                    V.bn_aggr(stat[:, tb * 2:tb * 2 + 2], mv[:])
                    V.drain()
                for tb in range(TB):
                    V.tensor_scalar(rstd4[:, tb:tb + 1],
                                    stat[:, tb * 2 + 1:tb * 2 + 2],
                                    EPS, None, OP.add)
                ev_stats = sV.inc(V.drain())
                wait(S, ev_stats)
                ev_sq = sS.inc(S.activation(rstd4[:], rstd4[:], AF.Sqrt,
                                            bias=0.0, scale=1.0))
                wait(V, ev_sq)
                V.reciprocal(rstd4[:], rstd4[:])
                V.drain()
                for tb in range(TB):
                    blk = wbuf[:, tb * D:(tb + 1) * D]
                    V.tensor_scalar(blk, blk, stat[:, tb * 2:tb * 2 + 1],
                                    rstd4[:, tb:tb + 1], OP.subtract, OP.mult)
                    V.drain()
                    V.tensor_tensor(blk, x_t[:, tb * D:(tb + 1) * D], blk, OP.add)
                    V.drain()
                ev_out = sV.inc(V.drain())
                ev_dma = store(out_d[t * TILE:(t + 1) * TILE, :].rearrange(
                    "(tb p) i -> p tb i", p=128),
                    wbuf.rearrange("p (tb i) -> p tb i", tb=TB), s_out, ev=ev_out)
                st8["x_free"] = ev_out
                st8["wbuf_free"] = ev_dma

            def emit_tile(xsrc_d, t, phase2, tap):
                ev_x = load(x_t.rearrange("p (tb i) -> p tb i", tb=TB),
                            xsrc_d[t * TILE:(t + 1) * TILE, :].rearrange(
                                "(tb p) i -> p tb i", p=128),
                            s_x, ev=st8["x_free"])
                wait(V, ev_x)
                ev_lnf = emit_ln(x_t)
                if tap and "lnf" in taps:
                    wait(V, ev_lnf)
                    for ic in range(NIC):
                        V.tensor_tensor(wbuf[:, ic * TILE:(ic + 1) * TILE],
                                        lnf_hi[:, ic * TILE:(ic + 1) * TILE],
                                        lnf_lo[:, ic * TILE:(ic + 1) * TILE],
                                        OP.add)
                    ev = sV.inc(V.drain())
                    ev = store(taps["lnf"].rearrange("(ic p) tt -> p ic tt", p=128),
                               wbuf[:, 0:NIC * TILE].rearrange(
                                   "p (ic tt) -> p ic tt", ic=NIC), s_tap, ev=ev)
                    st8["wbuf_free"] = ev
                    wait(S, ev); wait(V, ev)
                ev_v = [None] * (H // HG)
                ev_v[0] = emit_vproj(0, ev_lnf)
                pipe["last_evac"] = ev_v[0]
                for h in range(H):
                    emit_proj(h, ev_lnf, phase2)
                    if h > 0:
                        emit_attn(h - 1, (h - 1) % HG, ev_v[(h - 1) // HG],
                                  phase2, tap)
                    if h % HG == 0 and h > 0:
                        ev_v[h // HG] = emit_vproj(h // HG, ev_lnf)
                        pipe["last_evac"] = ev_v[h // HG]
                    emit_fmap(h, phase2, tap)
                emit_attn(H - 1, (H - 1) % HG, ev_v[(H - 1) // HG], phase2, tap)
                if phase2:
                    finish_o(tap)
                    emit_wo_ln2(t, tap)
                    ev_lnf3 = emit_ln(x_t)
                    emit_ffn(t, ev_lnf3)
                    emit_ln4_store(t, tap)
                else:
                    st8["x_free"] = (sV, sV.n)
                    st8["wbuf_free"] = (sT, sT.n)
                st8["lnf_free"] = (sT, sT.n)
                return ev_lnf

            for _s in (s_stg[0], s_stg[1], s_out):
                SY.wait_ge(_s.h, _s.n)

            for oc in range(NIC):
                ffn_oc(oc, None)
            for _s in (s_stg[0], s_stg[1]):
                SY.wait_ge(_s.h, _s.n)
                S.wait_ge(_s.h, _s.n)
                V.wait_ge(_s.h, _s.n)
            for t in range(np_tiles):
                emit_tile(xp_d, t, phase2=False, tap=False)
            for t in range(nt_tiles):
                emit_tile(xo_d, t, phase2=True, tap=(t == 0))

            for _s in dma_sems:
                SY.wait_ge(_s.h, _s.n)

    return nc


def kernel(**inputs):
    x = np.ascontiguousarray(np.asarray(inputs["L0_attn_x"], dtype=np.float32))
    nc = build()
    zeros = np.zeros((2048, D), np.float32)
    common = {
        "wq": np.ascontiguousarray(inputs["L0_attn_wq"]),
        "wk": np.ascontiguousarray(inputs["L0_attn_wk"]),
        "wv": np.ascontiguousarray(inputs["L0_attn_wv"]),
        "wo": np.ascontiguousarray(inputs["L0_attn_wo"]),
        "fq1w": np.ascontiguousarray(inputs["L0_attn_fq1_w"]),
        "fq2w": np.ascontiguousarray(inputs["L0_attn_fq2_w"]),
        "fk1w": np.ascontiguousarray(inputs["L0_attn_fk1_w"]),
        "fk2w": np.ascontiguousarray(inputs["L0_attn_fk2_w"]),
        "fq1b": np.ascontiguousarray(inputs["L0_attn_fq1_b"]),
        "fq2b": np.ascontiguousarray(inputs["L0_attn_fq2_b"]),
        "fk1b": np.ascontiguousarray(inputs["L0_attn_fk1_b"]),
        "fk2b": np.ascontiguousarray(inputs["L0_attn_fk2_b"]),
        "normw": np.ascontiguousarray(inputs["L0_attn_norm_w"]),
        "pw": np.ascontiguousarray(inputs["L0_ffn_pweight"]).reshape(D, D * 4),
        "nw": np.ascontiguousarray(inputs["L0_ffn_nweight"]).reshape(D, D * 4),
        "exps": np.ascontiguousarray(inputs["L0_ffn_exps"]),
        "bexps": np.ascontiguousarray(inputs["L0_ffn_bexps"]),
        "maskw": np.ascontiguousarray(inputs["L0_ffn_mask_weight"]),
        "fscale": np.ascontiguousarray(inputs["L0_ffn_scale"]),
        "bscale": np.ascontiguousarray(inputs["L0_ffn_biasscale"]),
        "pbias": np.ascontiguousarray(inputs["L0_ffn_pbias"]),
        "nbias": np.ascontiguousarray(inputs["L0_ffn_nbias"]),
    }
    in_maps = []
    for c in range(8):
        b, hf = c // 2, c % 2
        m = dict(common)
        m["xo"] = np.ascontiguousarray(x[b, hf * 2048:(hf + 1) * 2048])
        m["xp"] = zeros if hf == 0 else np.ascontiguousarray(x[b, 0:2048])
        in_maps.append(m)
    res = run_bass_kernel_spmd(nc, in_maps, core_ids=list(range(8)))
    out = np.empty((4, 4096, D), np.float32)
    for c in range(8):
        b, hf = c // 2, c % 2
        out[b, hf * 2048:(hf + 1) * 2048] = res.results[c]["out"]
    return out


# version 14
# speedup vs baseline: 1.0235x; 1.0235x over previous
import contextlib
import numpy as np
import ml_dtypes
import concourse.bass as bass
import concourse.mybir as mybir
from concourse.bass_utils import run_bass_kernel_spmd

F32 = mybir.dt.float32
BF16 = mybir.dt.bfloat16
AF = mybir.ActivationFunctionType
OP = mybir.AluOpType
AX = mybir.AxisListType

D = 2048
H = 16
DK = 128
CH = 64
TILE = 512
TB = TILE // 128
NIC = D // 128
NT = 4
NP = 4
HG = 4
SCALE = float(DK) ** -0.5
EPS = 1e-5

DBG_NT = None
DBG_NP = None
DBG_TAPS = ()


def build():
    np_tiles = NP if DBG_NP is None else DBG_NP
    nt_tiles = NT if DBG_NT is None else DBG_NT
    nc = bass.Bass("TRN2", target_bir_lowering=False)

    def din(name, shape):
        return nc.dram_tensor(name, shape, F32, kind="ExternalInput").ap()
    xo_d = din("xo", [2048, D])
    xp_d = din("xp", [2048, D])
    w_d = {"q": din("wq", [D, D]), "k": din("wk", [D, D]),
           "v": din("wv", [D, D]), "o": din("wo", [D, D])}
    fm_d = [din("fq1w", [DK, DK]), din("fq2w", [DK, DK]),
            din("fk1w", [DK, DK]), din("fk2w", [DK, DK])]
    fb_d = [din("fq1b", [DK]), din("fq2b", [DK]),
            din("fk1b", [DK]), din("fk2b", [DK])]
    normw_d = din("normw", [DK])
    pw_d = din("pw", [D, D * 4])
    nw_d = din("nw", [D, D * 4])
    exps_d = din("exps", [4]); bexps_d = din("bexps", [4])
    maskw_d = din("maskw", [4])
    fscale_d = din("fscale", [1]); bscale_d = din("bscale", [1])
    pbias_d = din("pbias", [D, 4]); nbias_d = din("nbias", [D, 4])
    out_d = nc.dram_tensor("out", [2048, D], F32, kind="ExternalOutput").ap()

    sc_wq = nc.dram_tensor("sc_wq", [H, NIC * 2, 128, DK], BF16,
                           kind="Internal").ap()
    sc_wk = nc.dram_tensor("sc_wk", [H, NIC * 2, 128, DK], BF16,
                           kind="Internal").ap()
    sc_wv = nc.dram_tensor("sc_wv", [NIC, 128, 2, D], BF16, kind="Internal").ap()
    sc_wo = nc.dram_tensor("sc_wo", [NIC, 128, 2, D], BF16, kind="Internal").ap()
    sc_wf = nc.dram_tensor("sc_wf", [NIC, 128, 2, D], BF16, kind="Internal").ap()
    sc_bias = nc.dram_tensor("sc_bias", [2, D], BF16, kind="Internal").ap()

    id128f_d = nc.inline_tensor(np.eye(128, dtype=np.float32), name="id128f").ap()
    trimask_d = nc.inline_tensor(np.triu(np.ones((128, 128))).astype(
        ml_dtypes.bfloat16), name="trimask").ap()

    taps = {}
    for tname in DBG_TAPS:
        shp = {"lnf": [D, TILE], "q0": [DK, TILE], "kp0": [DK, TILE],
               "state": [DK, DK], "attn": [TILE, D], "x2": [TILE, D],
               "y": [TILE, D]}[tname]
        taps[tname] = nc.dram_tensor("tap_" + tname, shp, F32,
                                     kind="ExternalOutput").ap()

    with contextlib.ExitStack() as stack:
        def sbuf(name, shape, dt=F32):
            return stack.enter_context(nc.sbuf_tensor(name, shape, dt))

        x_t = sbuf("x_t", [128, TB * D])
        wbuf = sbuf("wbuf", [128, TB * D])
        lnf_hi = sbuf("lnf_hi", [128, NIC * TILE], BF16)
        lnf_lo = sbuf("lnf_lo", [128, NIC * TILE], BF16)
        o_hi = sbuf("o_hi", [128, H * TILE], BF16)
        o_lo = sbuf("o_lo", [128, H * TILE], BF16)
        v_hi = sbuf("v_hi", [128, TB * HG * DK], BF16)
        v_lo = sbuf("v_lo", [128, TB * HG * DK], BF16)
        st_hi = sbuf("st_hi", [128, H * DK], BF16)
        st_lo = sbuf("st_lo", [128, H * DK], BF16)
        st_f = sbuf("st_f", [128, H * DK])
        wsq = sbuf("wsq", [128, NIC * 2 * DK], BF16)
        wsk = sbuf("wsk", [128, NIC * 2 * DK], BF16)
        wsv = sbuf("wsv", [128, 4 * 2 * 512], BF16)
        wst = sbuf("wst", [128, 4 * 2 * 512], BF16)
        q_hi = sbuf("q_hi", [128, TILE], BF16); q_lo = sbuf("q_lo", [128, TILE], BF16)
        k_hi = sbuf("k_hi", [128, TILE], BF16); k_lo = sbuf("k_lo", [128, TILE], BF16)
        qp_hi = sbuf("qp_hi", [128, TILE], BF16); qp_lo = sbuf("qp_lo", [128, TILE], BF16)
        kp_hi = sbuf("kp_hi", [128, TILE], BF16); kp_lo = sbuf("kp_lo", [128, TILE], BF16)
        kt_hi = sbuf("kt_hi", [128, TILE], BF16); kt_lo = sbuf("kt_lo", [128, TILE], BF16)
        sm_hi = sbuf("sm_hi", [128, 512], BF16); sm_lo = sbuf("sm_lo", [128, 512], BF16)
        fmw_hi = sbuf("fmw_hi", [128, 4 * DK], BF16)
        fmw_lo = sbuf("fmw_lo", [128, 4 * DK], BF16)
        fmb = sbuf("fmb", [128, 4])
        fbias = sbuf("fbias", [128, 4])
        nw_sb = sbuf("nw_sb", [128, 1])
        mv = sbuf("mv", [128, 24])
        stat = sbuf("stat", [128, 2 * TB])
        rstd4 = sbuf("rstd4", [128, TB])
        gn_s = sbuf("gn_s", [128, 4]); gn_r = sbuf("gn_r", [128, 4])
        cc = sbuf("cc", [128, 4]); ncc = sbuf("ncc", [128, 4])
        bs_c = sbuf("bs_c", [128, 1])
        bias_hl = sbuf("bias_hl", [1, 2 * D], BF16)
        id128f = sbuf("id128f_s", [128, 128])
        trimask = sbuf("trimask_s", [128, 128], BF16)
        ones_bf = sbuf("ones_bf", [1, 128], BF16)
        row4 = sbuf("row4", [1, 4]); row4b = sbuf("row4b", [1, 4])
        bs11 = sbuf("bs11", [1, 1]); fs11 = sbuf("fs11", [1, 1])
        b16hi = sbuf("b16hi", [128, 16], BF16); b16lo = sbuf("b16lo", [128, 16], BF16)

        fmstage = x_t[:, 0:128]
        on_f32 = wbuf[:, 2560:3072]
        b16r = x_t[:, 128:192]
        b16 = x_t[:, 256:272]; b16b = x_t[:, 272:288]
        sgn = x_t[:, 288:304]; sgn2 = x_t[:, 304:320]
        ones_f = x_t[0:1, 320:448]
        b16br = x_t[:, 192:256]
        pb = [stack.enter_context(nc.psum_tensor(f"pb{i}", [128, 512], F32))
              for i in range(8)]

        class Sem:
            def __init__(self, name):
                self.h = stack.enter_context(nc.semaphore(name))
                self.n = 0
            def inc(self, inst, k=1):
                inst.then_inc(self.h, k)
                self.n += k
                return (self, self.n)
            def now(self):
                return (self, self.n)
        s_misc = Sem("s_misc"); s_ld = Sem("s_ld"); s_x = Sem("s_x")
        s_wq = Sem("s_wq"); s_wk = Sem("s_wk")
        s_wv = [Sem("s_wv0"), Sem("s_wv1"), Sem("s_wv2"), Sem("s_wv3")]
        s_wt = [Sem("s_wt0"), Sem("s_wt1"), Sem("s_wt2"), Sem("s_wt3")]
        s_stg = [Sem("s_stg0"), Sem("s_stg1")]
        s_out = Sem("s_out"); s_tap = Sem("s_tap")
        s_ld2 = Sem("s_ld2")
        dma_sems = ([s_misc, s_ld, s_x, s_wq, s_wk, s_out, s_tap, s_ld2,
                     s_stg[0], s_stg[1]] + s_wv + s_wt)
        sT = Sem("sT"); sV = Sem("sV"); sS = Sem("sS")

        def wait(eng, ev):
            if ev is not None:
                eng.wait_ge(ev[0].h, ev[1])

        block = stack.enter_context(nc.Block())

        @block.sync
        def _(sync):
            T, V, S, G, SY = nc.tensor, nc.vector, nc.scalar, nc.gpsimd, nc.sync
            eps_c, scale_c, c23, inv128 = (None,) * 4

            def load(dst, src, sem, ev=None, eng=None):
                e = SY if eng is None else eng
                wait(e, ev)
                return sem.inc(e.dma_start(dst, src), 16)

            def load_ser(dst, src):
                ev = s_misc.inc(SY.dma_start(dst, src), 16)
                wait(SY, ev)
                return ev

            def store(dst, src, sem, ev=None, slow=False):
                wait(G, ev)
                return sem.inc(G.dma_start(dst, src,
                                           allow_slow_non_contiguous=slow), 16)

            def mm3(out, lT_hi, lT_lo, r_hi, r_lo, start, stop, tp=None):
                T.matmul(out, lT_hi, r_hi, start=start, stop=False, tile_position=tp)
                T.matmul(out, lT_hi, r_lo, start=False, stop=False, tile_position=tp)
                return T.matmul(out, lT_lo, r_hi, start=False, stop=stop,
                                tile_position=tp)

            def split(ps, hi, lo, scale_ap=None, wait_s=None, wait_v=None):
                wait(S, wait_s)
                if scale_ap is None:
                    ih = S.copy(hi, ps)
                else:
                    ih = S.activation(hi, ps, AF.Copy, bias=0.0, scale=scale_ap)
                eh = sS.inc(ih)
                wait(V, wait_v)
                wait(V, eh)
                il = V.scalar_tensor_tensor(
                    lo, ps, (1.0 if scale_ap is None else scale_ap), hi,
                    OP.mult, OP.subtract)
                return eh, sV.inc(il)

            def split_sbuf(src_f32, hi, lo, ev_src):
                wait(S, ev_src)
                eh = sS.inc(S.copy(hi, src_f32))
                wait(V, eh)
                il = V.tensor_tensor(lo, src_f32, hi, OP.subtract)
                return eh, sV.inc(V.drain())

            V.memset(st_hi[:], 0.0)
            V.memset(st_lo[:], 0.0)
            V.memset(st_f[:], 0.0)
            V.memset(ones_bf[:], 1.0)
            V.memset(ones_f[:], 1.0)
            ev_ms = sV.inc(V.drain())

            load_ser(id128f[:], id128f_d)
            load_ser(trimask[:], trimask_d)
            load_ser(nw_sb[:], normw_d.rearrange("(p one) -> p one", one=1))
            for j in range(4):
                ev_fbl = load_ser(fbias[:, j:j + 1],
                                  fb_d[j].rearrange("(p one) -> p one", one=1))

            ev_tr_prev = None
            ev_fm_lo = None
            for j in range(4):
                ev_l = load(fmstage[:], fm_d[j], s_ld,
                            ev=(ev_fm_lo if j == 1 else ev_tr_prev))
                wait(T, ev_l)
                wait(T, ev_ms)
                if j > 0:
                    wait(T, ev_fm_lo)
                ev_tr_prev = sT.inc(T.transpose(pb[0][:, 0:128], fmstage[:],
                                                id128f[:]))
                if j == 0:
                    wait(V, ev_tr_prev)
                    V.tensor_scalar(fmstage[:], pb[0][:, 0:128], SCALE,
                                    None, OP.mult)
                    ev_fs = sV.inc(V.drain())
                    _, ev_fm_lo = split_sbuf(fmstage[:],
                                             fmw_hi[:, 0:DK],
                                             fmw_lo[:, 0:DK], ev_fs)
                else:
                    _, ev_fm_lo = split(pb[0][:, 0:128],
                                        fmw_hi[:, j * DK:(j + 1) * DK],
                                        fmw_lo[:, j * DK:(j + 1) * DK],
                                        wait_s=ev_tr_prev)
            psum_gate2_init = ev_fm_lo
            wait(V, ev_fbl)
            V.tensor_scalar(fmb[:, 0:1], fbias[:, 0:1], SCALE, None, OP.mult)
            V.tensor_copy(fmb[:, 1:4], fbias[:, 1:4])
            sV.inc(V.drain())

            def stg_ap(buf):
                return lnf_hi[:, buf * 4096:(buf + 1) * 4096]
            wbuf_gate = [None, None]
            stg_gate = [None, None]

            psum_gate2 = [psum_gate2_init, None]

            def prep_weight_chunk(src_f32, which, oc, stg_override=None):
                buf = oc % 2
                pbase = 4 * buf
                stg = stg_ap(buf) if stg_override is None else stg_override
                wait(T, psum_gate2[buf])
                evt = []
                e = None
                for bk in range(4):
                    for jj in range(4):
                        ic = bk * 4 + jj
                        e = T.transpose(pb[pbase + bk][:, jj * 128:(jj + 1) * 128],
                                        src_f32[:, ic * 128:(ic + 1) * 128],
                                        id128f[:])
                    evt.append(sT.inc(e))
                ev_lo_all = None
                stg4 = stg.rearrange("p (ic hl o) -> p ic hl o", ic=NIC, hl=2)
                wait(S, stg_gate[buf])
                wait(V, stg_gate[buf])
                for bk in range(4):
                    hi_ap = stg4[:, bk * 4:(bk + 1) * 4, 0, :]
                    lo_ap = stg4[:, bk * 4:(bk + 1) * 4, 1, :]
                    sc = nw_sb[:, 0:1] if which == "o" else None
                    _, ev_lo_all = split(pb[pbase + bk][:, 0:512], hi_ap, lo_ap,
                                         scale_ap=sc, wait_s=evt[bk])
                psum_gate2[buf] = ev_lo_all
                if which in ("q", "k"):
                    dsc = sc_wq if which == "q" else sc_wk
                    dst = dsc[oc].rearrange("g p o -> p g o")
                    ev_w = store(dst, stg.rearrange("p (g o) -> p g o", g=2 * NIC),
                                 s_stg[buf], ev=ev_lo_all)
                else:
                    dsc = {"v": sc_wv, "o": sc_wo, "f": sc_wf}[which]
                    dst = dsc[:, :, :, oc * 128:(oc + 1) * 128].rearrange(
                        "ic p hl o -> p ic hl o")
                    store(dst[:, :, 0, :], stg4[:, :, 0, :], s_stg[buf],
                          ev=ev_lo_all)
                    ev_w = store(dst[:, :, 1, :], stg4[:, :, 1, :], s_stg[buf])
                stg_gate[buf] = ev_w
                return (evt[-1])

            tr_ev = [ev_tr_prev, ev_tr_prev]
            ldsem = [s_ld, s_x]
            gidx = 0
            for which in ("q", "k", "v", "o"):
                wd = w_d[which]
                for oc in range(NIC):
                    buf = gidx % 2
                    region = wbuf[:, buf * D:(buf + 1) * D]
                    ev_l = load(region, wd[oc * 128:(oc + 1) * 128, :],
                                ldsem[buf], ev=tr_ev[buf])
                    wait(T, ev_l)
                    tr_ev[buf] = prep_weight_chunk(region, which, oc)
                    gidx += 1

            ev1 = load_ser(row4[:], maskw_d.rearrange("(a b) -> a b", a=1))
            ev2 = load_ser(row4b[:], exps_d.rearrange("(a b) -> a b", a=1))
            ev3 = load_ser(fs11[:], fscale_d.rearrange("(a b) -> a b", a=1))
            wait(S, ev1)
            ev_sg1 = sS.inc(S.activation(row4[:], row4[:], AF.Sigmoid,
                                         bias=0.0, scale=1.0))
            wait(V, ev_sg1); wait(V, ev2); wait(V, ev3)
            V.tensor_tensor(row4[:], row4[:], row4b[:], OP.mult)
            V.drain()
            V.tensor_scalar(row4[:], row4[:], fs11[0:1, 0:1], None, OP.mult)
            ev_c = sV.inc(V.drain())
            wait(T, ev_c)
            wait(T, psum_gate2[0]); wait(T, psum_gate2[1])
            ev_cb = sT.inc(T.matmul(pb[4][:, 0:4], ones_f[:], row4[:],
                                    start=True, stop=True))
            wait(V, ev_cb)
            V.tensor_copy(cc[:], pb[4][:, 0:4])
            V.drain()
            V.tensor_scalar(ncc[:], cc[:], -1.0, None, OP.mult)
            ev_cc = sV.inc(V.drain())

            ffn_slot_gate = [(sT, 0), (sT, 0)]

            def ffn_oc(oc, ev_lnf_tile):
                pacc = x_t[:, 0:D]
                rows = slice(oc * 128, (oc + 1) * 128)
                slots = (wbuf[:, 4096:6144], wbuf[:, 6144:8192])
                qsem = (s_ld, s_ld2)
                for side, (wd_, co) in enumerate(((pw_d, cc), (nw_d, ncc))):
                    for qtr in range(4):
                        si = qtr % 2
                        slot = slots[si]
                        ev_l = load(slot, wd_[rows, qtr * 2048:(qtr + 1) * 2048],
                                    qsem[si], ev=ffn_slot_gate[si])
                        wait(S, ev_l)
                        wait(S, ev_lnf_tile)
                        if oc == 0 and side == 0 and qtr == 0:
                            wait(V, ev_cc)
                        ev_sg = sS.inc(S.activation(slot, slot, AF.Sigmoid,
                                                    bias=0.0, scale=1.0))
                        sp4 = slot.rearrange("p (i n) -> p n i", n=4)
                        dst = pacc[:, qtr * 512:(qtr + 1) * 512]
                        wait(V, ev_sg)
                        if side == 0:
                            V.tensor_scalar(dst, sp4[:, 0, :], co[:, 0:1],
                                            None, OP.mult)
                            V.drain()
                            nstart = 1
                        else:
                            nstart = 0
                        for n in range(nstart, 4):
                            V.scalar_tensor_tensor(dst, sp4[:, n, :],
                                                   co[:, n:n + 1], dst,
                                                   OP.mult, OP.add)
                            V.drain()
                        ffn_slot_gate[si] = sV.inc(V.drain())
                wait(T, ffn_slot_gate[0]); wait(T, ffn_slot_gate[1])
                wait(T, sS.now()); wait(T, sV.now())
                buf = oc % 2
                stg = o_hi[:, buf * 4096:(buf + 1) * 4096]
                ev_tr = prep_weight_chunk(pacc, "f", oc, stg_override=stg)
                st8["x_free"] = ev_tr
                st8["wbuf_free"] = ev_tr
            ev_pb_ = load_ser(b16r.rearrange("p (ic n) -> p ic n", n=4),
                              pbias_d.rearrange("(ic p) n -> p ic n", p=128))
            ev_nb_ = load_ser(b16br.rearrange("p (ic n) -> p ic n", n=4),
                              nbias_d.rearrange("(ic p) n -> p ic n", p=128))
            wait(SY, ev_cb)
            ev_be = load_ser(row4[:], bexps_d.rearrange("(a b) -> a b", a=1))
            wait(V, ev_pb_); wait(V, ev_nb_)
            V.tensor_tensor(b16br[:], b16r[:], b16br[:], OP.subtract)
            ev_d_ = sV.inc(V.drain())
            wait(T, ev_be)
            ev_bb = sT.inc(T.matmul(pb[5][:, 0:4], ones_f[:], row4[:],
                                    start=True, stop=True))
            wait(V, ev_bb)
            V.tensor_copy(fbias[:], pb[5][:, 0:4])
            V.drain()
            d4 = b16br.rearrange("p (ic n) -> p n ic", n=4)
            V.tensor_scalar(b16[:], d4[:, 0, :], fbias[:, 0:1], None, OP.mult)
            V.drain()
            for n in range(1, 4):
                V.scalar_tensor_tensor(b16[:], d4[:, n, :], fbias[:, n:n + 1],
                                       b16[:], OP.mult, OP.add)
                V.drain()
            V.tensor_scalar(b16[:], b16[:], 1.0, None, OP.min)
            V.drain()
            V.tensor_scalar(b16[:], b16[:], -1.0, None, OP.max)
            ev_clip = sV.inc(V.drain())
            wait(S, ev_clip)
            ev_sgn = sS.inc(S.activation(sgn[:], b16[:], AF.Sign, bias=0.0,
                                         scale=1.0))
            ev_ab = sS.inc(S.activation(b16b[:], b16[:], AF.Abs, bias=0.0,
                                        scale=15.0))
            wait(V, ev_ab)
            V.tensor_scalar(sgn2[:], b16b[:], 0.5, None, OP.is_ge)
            V.drain()
            for kk in range(2, 16):
                V.scalar_tensor_tensor(sgn2[:], b16b[:], kk - 0.5, sgn2[:],
                                       OP.is_ge, OP.add)
                V.drain()
            V.tensor_copy(b16b[:], sgn2[:])
            V.drain()
            wait(V, ev_sgn)
            V.tensor_tensor(b16b[:], b16b[:], sgn[:], OP.mult)
            ev_bs0 = sV.inc(V.drain())
            ev_bsl = load_ser(bs11[:], bscale_d.rearrange("(a b) -> a b", a=1))
            wait(T, ev_bsl)
            ev_bsb = sT.inc(T.matmul(pb[6][:, 0:1], ones_f[:], bs11[:],
                                     start=True, stop=True))
            wait(V, ev_bsb)
            V.tensor_copy(bs_c[:], pb[6][:, 0:1])
            V.drain()
            V.tensor_scalar(bs_c[:], bs_c[:], 1.0 / 15.0, None, OP.mult)
            V.drain()
            wait(V, ev_bs0)
            V.tensor_scalar(b16b[:], b16b[:], bs_c[:, 0:1], None, OP.mult)
            V.drain()
            V.tensor_copy(b16hi[:], b16b[:])
            V.drain()
            V.tensor_tensor(b16lo[:], b16b[:], b16hi[:], OP.subtract)
            ev_bv = sV.inc(V.drain())
            ev_bw0 = store(sc_bias[0].rearrange("(ic p) -> p ic", p=128),
                           b16hi[:], s_out, ev=ev_bv, slow=True)
            wait(G, ev_bw0)
            ev_bw = store(sc_bias[1].rearrange("(ic p) -> p ic", p=128), b16lo[:],
                          s_out, slow=True)
            wait(SY, ev_bw)
            ev_bload = load_ser(bias_hl[:],
                                sc_bias.rearrange("a b -> (a b)").rearrange(
                                    "(one ab) -> one ab", one=1))

            st8 = {
                "x_free": None, "wbuf_free": ev_bload, "lnf_free": None,
                "v_free": None, "wsv_gate": [None] * 4, "wst_gate": [None] * 4,
                "stage_q_free": None, "stage_k_free": None,
                "st_ev": [ev_ms] * H,
            }
            ln_psum_gate = [None, None]

            def emit_ln(src):
                for tb in range(TB):
                    blk = src[:, tb * D:(tb + 1) * D]
                    for j in range(4):
                        V.bn_stats(mv[:, j * 6:(j + 1) * 6],
                                   blk[:, j * 512:(j + 1) * 512])
                    V.drain()
                    V.bn_aggr(stat[:, tb * 2:tb * 2 + 2], mv[:])
                    V.drain()
                    V.tensor_scalar(rstd4[:, tb:tb + 1],
                                    stat[:, tb * 2 + 1:tb * 2 + 2],
                                    EPS, None, OP.add)
                ev_stats = sV.inc(V.drain())
                wait(S, ev_stats)
                ev_sq = sS.inc(S.activation(rstd4[:], rstd4[:], AF.Sqrt,
                                            bias=0.0, scale=1.0))
                wait(V, ev_sq)
                V.reciprocal(rstd4[:], rstd4[:])
                V.drain()
                wait(V, st8["wbuf_free"])
                ev_ap = [None] * TB
                for tb in range(TB):
                    V.tensor_scalar(wbuf[:, tb * D:(tb + 1) * D],
                                    src[:, tb * D:(tb + 1) * D],
                                    stat[:, tb * 2:tb * 2 + 1],
                                    rstd4[:, tb:tb + 1], OP.subtract, OP.mult)
                    ev_ap[tb] = sV.inc(V.drain())
                wait(T, st8["lnf_free"])
                wait(T, sS.now()); wait(T, sV.now())
                ev_lnf = None
                for tb in range(TB):
                    wait(T, ev_ap[tb])
                    for grp in range(4):
                        bank = pb[4 + (grp % 2)]
                        wait(T, ln_psum_gate[grp % 2])
                        e = None
                        for j in range(4):
                            ic = grp * 4 + j
                            e = T.transpose(bank[:, j * 128:(j + 1) * 128],
                                            wbuf[:, tb * D + ic * 128:
                                                 tb * D + (ic + 1) * 128],
                                            id128f[:])
                        evt = sT.inc(e)
                        hi4 = lnf_hi.rearrange("p (ic tt) -> p ic tt", ic=NIC)[
                            :, grp * 4:(grp + 1) * 4, tb * 128:(tb + 1) * 128]
                        lo4 = lnf_lo.rearrange("p (ic tt) -> p ic tt", ic=NIC)[
                            :, grp * 4:(grp + 1) * 4, tb * 128:(tb + 1) * 128]
                        wait(S, evt)
                        eh = sS.inc(S.copy(hi4, bank[:, 0:512].rearrange(
                            "p (j c) -> p j c", j=4)))
                        wait(V, eh)
                        il = V.scalar_tensor_tensor(
                            lo4, bank[:, 0:512].rearrange("p (j c) -> p j c",
                                                          j=4),
                            1.0, hi4, OP.mult, OP.subtract)
                        ev_lnf = sV.inc(il)
                        ln_psum_gate[grp % 2] = ev_lnf
                return ev_lnf

            hs = [dict() for _ in range(H)]
            pipe = {"pending_o": None, "last_st": None, "last_kt": None,
                    "last_qpread": None, "last_evac": None, "last_osplit": None,
                    "last_vread": None, "last_ot": None}

            def emit_vproj(g, ev_lnf):
                def wv_load(ic, gate):
                    buf = ic % 4
                    dst = wsv[:, buf * 1024:(buf + 1) * 1024]
                    return load(dst.rearrange("p (hl o) -> p hl o", hl=2),
                                sc_wv[ic, :, :, g * 512:(g + 1) * 512],
                                s_wv[buf], ev=gate)
                wv_ev = [None] * NIC
                for jj in range(4):
                    wv_ev[jj] = wv_load(jj, st8["wsv_gate"][jj])
                wait(T, ev_lnf)
                wait(T, pipe["last_evac"])
                ev_last = None
                for half in range(2):
                    banks = (pb[4], pb[7])
                    if half == 1:
                        wait(T, ev_last)
                    last = [None, None]
                    for ic in range(NIC):
                        buf = ic % 4
                        wait(T, wv_ev[ic])
                        r_hi = wsv[:, buf * 1024:buf * 1024 + 512]
                        r_lo = wsv[:, buf * 1024 + 512:(buf + 1) * 1024]
                        for j in range(2):
                            tb = half * 2 + j
                            lt_hi = lnf_hi[:, ic * TILE + tb * 128:
                                           ic * TILE + (tb + 1) * 128]
                            lt_lo = lnf_lo[:, ic * TILE + tb * 128:
                                           ic * TILE + (tb + 1) * 128]
                            last[j] = mm3(banks[j][:, 0:512], lt_hi, lt_lo,
                                          r_hi, r_lo, start=(ic == 0),
                                          stop=(ic == NIC - 1))
                        ev_ic = sT.inc(last[1])
                        if half == 0:
                            wv_ev[(ic + 4) % NIC] = wv_load((ic + 4) % NIC,
                                                            ev_ic)
                        elif ic + 4 < NIC:
                            wv_ev[ic + 4] = wv_load(ic + 4, ev_ic)
                        if half == 1 and ic >= NIC - 4:
                            st8["wsv_gate"][buf] = ev_ic
                    ev_mm = (sT, sT.n)
                    if half == 0:
                        wait(S, pipe["last_vread"])
                        wait(V, pipe["last_vread"])
                    for j in range(2):
                        tb = half * 2 + j
                        _, ev_last = split(banks[j][:, 0:512],
                                           v_hi[:, tb * 512:(tb + 1) * 512],
                                           v_lo[:, tb * 512:(tb + 1) * 512],
                                           wait_s=ev_mm)
                return ev_last

            def bankpair(h):
                return (pb[0], pb[1]) if h % 2 == 0 else (pb[2], pb[3])

            def proj_mms(ws, bank, ev_w, ev_lnf):
                wait(T, ev_w)
                wait(T, ev_lnf)
                last = None
                for ic in range(NIC):
                    lT_hi = ws[:, ic * 256:ic * 256 + 128]
                    lT_lo = ws[:, ic * 256 + 128:(ic + 1) * 256]
                    r_hi = lnf_hi[:, ic * TILE:(ic + 1) * TILE]
                    r_lo = lnf_lo[:, ic * TILE:(ic + 1) * TILE]
                    last = mm3(bank[:, 0:512], lT_hi, lT_lo, r_hi, r_lo,
                               start=(ic == 0), stop=(ic == NIC - 1))
                return sT.inc(last)

            def emit_proj(h, ev_lnf, phase2):
                bk, bq = bankpair(h)
                if phase2:
                    wsk_h, k_sem, k_gate_key = wsk, s_wk, "stage_k_free"
                elif h % 2 == 0:
                    wsk_h, k_sem, k_gate_key = wsk, s_wk, "stage_k_free"
                else:
                    wsk_h, k_sem, k_gate_key = wsq, s_wq, "stage_q_free"
                ev_wk = load(wsk_h.rearrange("p (g o) -> p g o", g=2 * NIC),
                             sc_wk[h].rearrange("g p o -> p g o"),
                             k_sem, ev=st8[k_gate_key])
                if phase2:
                    ev_wq = load(wsq.rearrange("p (g o) -> p g o", g=2 * NIC),
                                 sc_wq[h].rearrange("g p o -> p g o"),
                                 s_wq, ev=st8["stage_q_free"])
                ev_kp = proj_mms(wsk_h, bk, ev_wk, ev_lnf)
                st8[k_gate_key] = ev_kp
                if phase2:
                    ev_qp = proj_mms(wsq, bq, ev_wq, ev_lnf)
                    st8["stage_q_free"] = ev_qp
                _, ev_kl = split(bk[:, 0:512], k_hi[:], k_lo[:], wait_s=ev_kp)
                hs[h]["kl"] = ev_kl
                if phase2:
                    _, ev_ql = split(bq[:, 0:512], q_hi[:], q_lo[:],
                                     wait_s=ev_qp)
                    hs[h]["ql"] = ev_ql

            def fmap(idx1, idx2, src_hi, src_lo, dst_f32, bank1, bank2,
                     read_gate):
                f1 = mm3(bank1[:, 0:512], fmw_hi[:, idx1 * DK:(idx1 + 1) * DK],
                         fmw_lo[:, idx1 * DK:(idx1 + 1) * DK],
                         src_hi, src_lo, True, True)
                f2 = mm3(bank2[:, 0:512], fmw_hi[:, idx2 * DK:(idx2 + 1) * DK],
                         fmw_lo[:, idx2 * DK:(idx2 + 1) * DK],
                         src_hi, src_lo, True, True)
                ev_f = sT.inc(f2)
                wait(S, ev_f)
                wait(S, last_fmap_mult[0])
                f1b = wbuf[:, 0:512]
                ev_s1 = sS.inc(S.activation(f1b, bank1[:, 0:512], AF.Identity,
                                            bias=fmb[:, idx1:idx1 + 1],
                                            scale=1.0))
                wait(V, ev_s1)
                wait(V, read_gate)
                V.scalar_tensor_tensor(dst_f32, bank2[:, 0:512],
                                       fmb[:, idx2:idx2 + 1], f1b,
                                       OP.add, OP.mult)
                ev = sV.inc(V.drain())
                last_fmap_mult[0] = ev
                return ev

            last_fmap_mult = [None]

            def emit_fmap(h, phase2, tap):
                bk, bq = bankpair(h)
                kp_f32 = wbuf[:, 1536:2048]
                wait(T, hs[h]["kl"])
                if phase2:
                    wait(T, hs[h]["ql"])
                ev_kpf = fmap(2, 3, k_hi[:], k_lo[:], kp_f32, bk, bq,
                              pipe["last_kt"])
                if phase2:
                    _, ev_kpf = split_sbuf2(kp_f32, kp_hi[:], kp_lo[:],
                                            ev_kpf, pipe["last_qpread"])
                hs[h]["kpf"] = ev_kpf
                if phase2:
                    qp_f32 = wbuf[:, 1024:1536]
                    wait(T, hs[h]["ql"])
                    wait(T, pipe["last_evac"])
                    ev_qpf = fmap(0, 1, q_hi[:], q_lo[:], qp_f32, pb[4], pb[7],
                                  None)
                    _, ev_qsp = split_sbuf2(qp_f32, qp_hi[:], qp_lo[:],
                                            ev_qpf, pipe["last_qpread"])
                    hs[h]["qsp"] = ev_qsp
                    if tap and h == 0 and "q0" in taps:
                        wait(V, ev_qsp)
                        V.tensor_tensor(wbuf[:, 3072:3584], qp_hi[:], qp_lo[:],
                                        OP.add)
                        ev = sV.inc(V.drain())
                        store(taps["q0"], wbuf[:, 3072:3584], s_tap, ev=ev)
                    if tap and h == 0 and "kp0" in taps:
                        wait(V, ev_qsp)
                        wait(V, s_tap.now())
                        V.tensor_tensor(wbuf[:, 3072:3584], kp_hi[:], kp_lo[:],
                                        OP.add)
                        ev = sV.inc(V.drain())
                        store(taps["kp0"], wbuf[:, 3072:3584], s_tap, ev=ev)

            def split_sbuf2(src_f32, hi, lo, ev_src, gate):
                wait(S, ev_src)
                wait(S, gate)
                eh = sS.inc(S.copy(hi, src_f32))
                wait(V, eh)
                wait(V, gate)
                V.tensor_tensor(lo, src_f32, hi, OP.subtract)
                return eh, sV.inc(V.drain())

            def state_update(h, ev_kv):
                sl = slice(h * DK, (h + 1) * DK)
                wait(V, ev_kv)
                V.tensor_tensor(st_f[:, sl], st_f[:, sl], pb[5][:, 0:128],
                                OP.add)
                ev_a = sV.inc(V.drain())
                wait(S, ev_a)
                evh = sS.inc(S.copy(st_hi[:, sl], st_f[:, sl]))
                wait(V, evh)
                V.tensor_tensor(st_lo[:, sl], st_f[:, sl], st_hi[:, sl],
                                OP.subtract)
                return sV.inc(V.drain())

            def finish_o(tap):
                hp = pipe["pending_o"]
                if hp is None:
                    return
                pipe["pending_o"] = None
                wait(T, hs[hp]["on"])
                e = None
                for cb in range(TB):
                    e = T.transpose(pb[6][:, cb * 128:(cb + 1) * 128],
                                    on_f32[:, cb * 128:(cb + 1) * 128],
                                    id128f[:])
                ev_ot = sT.inc(e)
                pipe["last_ot"] = ev_ot
                _, ev_os = split(pb[6][:, 0:512],
                                 o_hi[:, hp * TILE:(hp + 1) * TILE],
                                 o_lo[:, hp * TILE:(hp + 1) * TILE],
                                 wait_s=ev_ot)
                pipe["last_osplit"] = ev_os

            def emit_attn(h, hh, ev_v, phase2, tap):
                if phase2:
                    finish_o(tap)
                kp_f32 = wbuf[:, 1536:2048]
                wait(T, hs[h]["kpf"])
                wait(T, pipe["last_st"])
                e = None
                for tb in range(TB):
                    e = T.transpose(pb[5][:, tb * 128:(tb + 1) * 128],
                                    kp_f32[:, tb * 128:(tb + 1) * 128],
                                    id128f[:])
                ev_t = sT.inc(e)
                pipe["last_kt"] = ev_t
                _, ev_ktl = split(pb[5][:, 0:512], kt_hi[:], kt_lo[:],
                                  wait_s=ev_t)

                if not phase2:
                    wait(T, ev_ktl)
                    wait(T, ev_v)
                    wait(T, st8["st_ev"][h])
                    last = None
                    for tb in range(TB):
                        vcol = tb * 512 + hh * 128
                        last = mm3(pb[5][:, 0:128],
                                   kt_hi[:, tb * 128:(tb + 1) * 128],
                                   kt_lo[:, tb * 128:(tb + 1) * 128],
                                   v_hi[:, vcol:vcol + 128],
                                   v_lo[:, vcol:vcol + 128],
                                   start=(tb == 0), stop=(tb == TB - 1))
                    ev_kv = sT.inc(last)
                    st8["st_ev"][h] = state_update(h, ev_kv)
                    pipe["last_st"] = st8["st_ev"][h]
                    return

                smst = wbuf[:, 2048:2560]
                ev_evac = [None] * TB
                ev_sc = [None] * TB

                def emit_scores(cb):
                    bank = pb[4] if cb % 2 == 0 else pb[7]
                    lastm = None
                    for mb in range(cb + 1):
                        lastm = mm3(bank[:, mb * 128:(mb + 1) * 128],
                                    kp_hi[:, mb * 128:(mb + 1) * 128],
                                    kp_lo[:, mb * 128:(mb + 1) * 128],
                                    qp_hi[:, cb * 128:(cb + 1) * 128],
                                    qp_lo[:, cb * 128:(cb + 1) * 128],
                                    start=(mb == 0), stop=(mb == cb))
                    ev_sc[cb] = sT.inc(lastm)

                def emit_evac(cb, ev_o_prev=None):
                    bank = pb[4] if cb % 2 == 0 else pb[7]
                    wait(S, ev_o_prev)
                    wait(V, ev_o_prev)
                    if cb > 0:
                        wait(S, ev_sc[cb])
                        eh = sS.inc(S.copy(sm_hi[:, 0:cb * 128],
                                           bank[:, 0:cb * 128]))
                        wait(V, eh)
                        V.scalar_tensor_tensor(sm_lo[:, 0:cb * 128],
                                               bank[:, 0:cb * 128], 1.0,
                                               sm_hi[:, 0:cb * 128],
                                               OP.mult, OP.subtract)
                        V.drain()
                    wait(V, ev_sc[cb])
                    V.tensor_tensor(smst[0:128, 0:128],
                                    bank[:, cb * 128:(cb + 1) * 128],
                                    trimask[:], OP.mult)
                    V.drain()
                    V.tensor_copy(sm_hi[:, cb * 128:(cb + 1) * 128],
                                  smst[0:128, 0:128])
                    V.drain()
                    V.tensor_tensor(sm_lo[:, cb * 128:(cb + 1) * 128],
                                    smst[0:128, 0:128],
                                    sm_hi[:, cb * 128:(cb + 1) * 128],
                                    OP.subtract)
                    ev_evac[cb] = sV.inc(V.drain())

                def emit_o(cb):
                    wait(T, ev_evac[cb])
                    if cb == 0:
                        wait(T, pipe["last_osplit"])
                        wait(T, ev_v)
                        wait(T, st8["st_ev"][h])
                    ocol = cb * 128
                    mm3(pb[6][:, ocol:ocol + 128],
                        qp_hi[:, cb * 128:(cb + 1) * 128],
                        qp_lo[:, cb * 128:(cb + 1) * 128],
                        st_hi[:, h * DK:(h + 1) * DK],
                        st_lo[:, h * DK:(h + 1) * DK],
                        start=(cb == 0), stop=False)
                    last = None
                    for mb in range(cb + 1):
                        vcol = mb * 512 + hh * 128
                        last = mm3(pb[6][:, ocol:ocol + 128],
                                   sm_hi[:, mb * 128:(mb + 1) * 128],
                                   sm_lo[:, mb * 128:(mb + 1) * 128],
                                   v_hi[:, vcol:vcol + 128],
                                   v_lo[:, vcol:vcol + 128],
                                   start=False, stop=(cb == TB - 1 and
                                                      mb == cb))
                    return sT.inc(last)

                wait(T, hs[h]["qsp"])
                emit_scores(0)
                emit_evac(0)
                emit_scores(1)
                ev_o = None
                for cb in range(TB):
                    ev_o = emit_o(cb)
                    if cb + 1 <= TB - 1:
                        emit_evac(cb + 1, ev_o)
                    if cb + 2 <= TB - 1:
                        emit_scores(cb + 2)
                pipe["last_qpread"] = ev_o
                pipe["last_evac"] = ev_evac[TB - 1]
                wait(T, ev_ktl)
                last = None
                for tb in range(TB):
                    vcol = tb * 512 + hh * 128
                    last = mm3(pb[5][:, 0:128],
                               kt_hi[:, tb * 128:(tb + 1) * 128],
                               kt_lo[:, tb * 128:(tb + 1) * 128],
                               v_hi[:, vcol:vcol + 128],
                               v_lo[:, vcol:vcol + 128],
                               start=(tb == 0), stop=(tb == TB - 1))
                ev_kv = sT.inc(last)
                pipe["last_vread"] = ev_kv
                st8["st_ev"][h] = state_update(h, ev_kv)
                pipe["last_st"] = st8["st_ev"][h]
                if tap and h == 0 and "state" in taps:
                    wait(V, st8["st_ev"][h])
                    wait(V, s_tap.now())
                    V.tensor_tensor(wbuf[:, 3072:3200], st_hi[:, 0:DK],
                                    st_lo[:, 0:DK], OP.add)
                    ev = sV.inc(V.drain())
                    store(taps["state"], wbuf[:, 3072:3200], s_tap, ev=ev)

                wait(S, ev_kv)
                ev_sq2 = sS.inc(S.activation(on_f32[:], pb[6][:, 0:512],
                                             AF.Square, bias=0.0, scale=1.0))
                wait(V, ev_sq2)
                wait(V, pipe["last_ot"])
                V.tensor_reduce(gn_s[:],
                                on_f32.rearrange("p (c e) -> p c e", e=128),
                                AX.X, OP.add)
                V.drain()
                V.tensor_scalar(gn_r[:], gn_s[:], 1.0 / DK, EPS,
                                OP.mult, OP.add)
                ev_rd2 = sV.inc(V.drain())
                wait(S, ev_rd2)
                ev_sq3 = sS.inc(S.activation(gn_r[:], gn_r[:], AF.Sqrt,
                                             bias=0.0, scale=1.0))
                wait(V, ev_sq3)
                V.reciprocal(gn_r[:], gn_r[:])
                V.drain()
                for cb in range(TB):
                    V.tensor_scalar(on_f32[:, cb * 128:(cb + 1) * 128],
                                    pb[6][:, cb * 128:(cb + 1) * 128],
                                    gn_r[:, cb:cb + 1], None, OP.mult)
                hs[h]["on"] = sV.inc(V.drain())
                pipe["pending_o"] = h

            def emit_wo_ln2(t, tap):
                wait(T, sV.now()); wait(T, sS.now())
                if taps:
                    wait(S, s_tap.now()); wait(V, s_tap.now())
                for fb in range(4):
                    if fb > 0:
                        wait(T, sS.now())
                    for h in range(H):
                        buf = (fb * H + h) % 4
                        dst = wst[:, buf * 1024:(buf + 1) * 1024]
                        e = load(dst.rearrange("p (hl o) -> p hl o", hl=2),
                                 sc_wo[h, :, :, fb * 512:(fb + 1) * 512],
                                 s_wt[buf], ev=st8["wst_gate"][buf])
                        wait(T, e)
                        r_hi = dst[:, 0:512]; r_lo = dst[:, 512:1024]
                        last = None
                        for tb in range(TB):
                            lT_hi = o_hi[:, h * TILE + tb * 128:
                                         h * TILE + (tb + 1) * 128]
                            lT_lo = o_lo[:, h * TILE + tb * 128:
                                         h * TILE + (tb + 1) * 128]
                            last = mm3(pb[tb][:, 0:512], lT_hi, lT_lo,
                                       r_hi, r_lo, start=(h == 0),
                                       stop=(h == H - 1))
                        st8["wst_gate"][buf] = sT.inc(last)
                    ev_mm = (sT, sT.n)
                    wait(S, ev_mm)
                    for tb in range(TB):
                        sS.inc(S.copy(
                            wbuf[:, tb * D + fb * 512:tb * D + (fb + 1) * 512],
                            pb[tb][:, 0:512]))
                ev_ao = sS.now()
                wait(V, ev_ao)
                if tap and "attn" in taps:
                    ev = store(taps["attn"].rearrange("(tb p) i -> p tb i", p=128),
                               wbuf.rearrange("p (tb i) -> p tb i", tb=TB),
                               s_tap, ev=ev_ao)
                    wait(V, ev)
                for tb in range(TB):
                    blk = wbuf[:, tb * D:(tb + 1) * D]
                    for j in range(4):
                        V.bn_stats(mv[:, j * 6:(j + 1) * 6],
                                   blk[:, j * 512:(j + 1) * 512])
                    V.drain()
                    V.bn_aggr(stat[:, tb * 2:tb * 2 + 2], mv[:])
                    V.drain()
                for tb in range(TB):
                    V.tensor_scalar(rstd4[:, tb:tb + 1],
                                    stat[:, tb * 2 + 1:tb * 2 + 2],
                                    EPS, None, OP.add)
                ev_stats = sV.inc(V.drain())
                wait(S, ev_stats)
                ev_sq = sS.inc(S.activation(rstd4[:], rstd4[:], AF.Sqrt,
                                            bias=0.0, scale=1.0))
                wait(V, ev_sq)
                V.reciprocal(rstd4[:], rstd4[:])
                V.drain()
                for tb in range(TB):
                    blk = wbuf[:, tb * D:(tb + 1) * D]
                    V.tensor_scalar(blk, blk, stat[:, tb * 2:tb * 2 + 1],
                                    rstd4[:, tb:tb + 1], OP.subtract, OP.mult)
                    V.drain()
                    V.tensor_tensor(x_t[:, tb * D:(tb + 1) * D],
                                    x_t[:, tb * D:(tb + 1) * D], blk, OP.add)
                    V.drain()
                ev_x2 = sV.inc(V.drain())
                st8["wbuf_free"] = ev_x2
                if tap and "x2" in taps:
                    store(taps["x2"].rearrange("(tb p) i -> p tb i", p=128),
                          x_t.rearrange("p (tb i) -> p tb i", tb=TB), s_tap,
                          ev=ev_x2)

            def emit_ffn(t, ev_lnf3):
                wait(T, st8["wbuf_free"])
                wait(T, ev_lnf3)
                for fb in range(4):
                    if fb > 0:
                        wait(T, sS.now())
                    for tb in range(TB):
                        T.matmul(pb[tb][:, 0:512], ones_bf[:],
                                 bias_hl[0:1, fb * 512:(fb + 1) * 512],
                                 start=True, stop=False)
                        T.matmul(pb[tb][:, 0:512], ones_bf[:],
                                 bias_hl[0:1, D + fb * 512:D + (fb + 1) * 512],
                                 start=False, stop=False)
                    for ic in range(NIC):
                        buf = (fb * NIC + ic) % 2
                        dst = wst[:, buf * 1024:(buf + 1) * 1024]
                        e = load(dst.rearrange("p (hl o) -> p hl o", hl=2),
                                 sc_wf[ic, :, :, fb * 512:(fb + 1) * 512],
                                 s_wt[buf], ev=st8["wst_gate"][buf])
                        wait(T, e)
                        r_hi = dst[:, 0:512]; r_lo = dst[:, 512:1024]
                        last = None
                        for tb in range(TB):
                            lT_hi = lnf_hi[:, ic * TILE + tb * 128:
                                           ic * TILE + (tb + 1) * 128]
                            lT_lo = lnf_lo[:, ic * TILE + tb * 128:
                                           ic * TILE + (tb + 1) * 128]
                            last = mm3(pb[tb][:, 0:512], lT_hi, lT_lo,
                                       r_hi, r_lo, start=False,
                                       stop=(ic == NIC - 1))
                        st8["wst_gate"][buf] = sT.inc(last)
                    ev_mm = (sT, sT.n)
                    wait(S, ev_mm)
                    for tb in range(TB):
                        sS.inc(S.copy(
                            wbuf[:, tb * D + fb * 512:tb * D + (fb + 1) * 512],
                            pb[tb][:, 0:512]))

            def emit_ln4_store(t, tap):
                ev_y = sS.now()
                wait(V, ev_y)
                if tap and "y" in taps:
                    ev = store(taps["y"].rearrange("(tb p) i -> p tb i", p=128),
                               wbuf.rearrange("p (tb i) -> p tb i", tb=TB),
                               s_tap, ev=ev_y)
                    wait(V, ev)
                for tb in range(TB):
                    blk = wbuf[:, tb * D:(tb + 1) * D]
                    for j in range(4):
                        V.bn_stats(mv[:, j * 6:(j + 1) * 6],
                                   blk[:, j * 512:(j + 1) * 512])
                    V.drain()
                    V.bn_aggr(stat[:, tb * 2:tb * 2 + 2], mv[:])
                    V.drain()
                for tb in range(TB):
                    V.tensor_scalar(rstd4[:, tb:tb + 1],
                                    stat[:, tb * 2 + 1:tb * 2 + 2],
                                    EPS, None, OP.add)
                ev_stats = sV.inc(V.drain())
                wait(S, ev_stats)
                ev_sq = sS.inc(S.activation(rstd4[:], rstd4[:], AF.Sqrt,
                                            bias=0.0, scale=1.0))
                wait(V, ev_sq)
                V.reciprocal(rstd4[:], rstd4[:])
                V.drain()
                for tb in range(TB):
                    blk = wbuf[:, tb * D:(tb + 1) * D]
                    V.tensor_scalar(blk, blk, stat[:, tb * 2:tb * 2 + 1],
                                    rstd4[:, tb:tb + 1], OP.subtract, OP.mult)
                    V.drain()
                    V.tensor_tensor(blk, x_t[:, tb * D:(tb + 1) * D], blk, OP.add)
                    V.drain()
                ev_out = sV.inc(V.drain())
                ev_dma = store(out_d[t * TILE:(t + 1) * TILE, :].rearrange(
                    "(tb p) i -> p tb i", p=128),
                    wbuf.rearrange("p (tb i) -> p tb i", tb=TB), s_out, ev=ev_out)
                st8["x_free"] = ev_out
                st8["wbuf_free"] = ev_dma

            def emit_tile(xsrc_d, t, phase2, tap):
                ev_x = load(x_t.rearrange("p (tb i) -> p tb i", tb=TB),
                            xsrc_d[t * TILE:(t + 1) * TILE, :].rearrange(
                                "(tb p) i -> p tb i", p=128),
                            s_x, ev=st8["x_free"])
                wait(V, ev_x)
                ev_lnf = emit_ln(x_t)
                if tap and "lnf" in taps:
                    wait(V, ev_lnf)
                    for ic in range(NIC):
                        V.tensor_tensor(wbuf[:, ic * TILE:(ic + 1) * TILE],
                                        lnf_hi[:, ic * TILE:(ic + 1) * TILE],
                                        lnf_lo[:, ic * TILE:(ic + 1) * TILE],
                                        OP.add)
                    ev = sV.inc(V.drain())
                    ev = store(taps["lnf"].rearrange("(ic p) tt -> p ic tt", p=128),
                               wbuf[:, 0:NIC * TILE].rearrange(
                                   "p (ic tt) -> p ic tt", ic=NIC), s_tap, ev=ev)
                    st8["wbuf_free"] = ev
                    wait(S, ev); wait(V, ev)
                ev_v = [None] * (H // HG)
                ev_v[0] = emit_vproj(0, ev_lnf)
                pipe["last_evac"] = ev_v[0]
                for h in range(H):
                    emit_proj(h, ev_lnf, phase2)
                    if h > 0:
                        emit_attn(h - 1, (h - 1) % HG, ev_v[(h - 1) // HG],
                                  phase2, tap)
                    if h % HG == 0 and h > 0:
                        ev_v[h // HG] = emit_vproj(h // HG, ev_lnf)
                        pipe["last_evac"] = ev_v[h // HG]
                    emit_fmap(h, phase2, tap)
                emit_attn(H - 1, (H - 1) % HG, ev_v[(H - 1) // HG], phase2, tap)
                if phase2:
                    finish_o(tap)
                    emit_wo_ln2(t, tap)
                    ev_lnf3 = emit_ln(x_t)
                    emit_ffn(t, ev_lnf3)
                    emit_ln4_store(t, tap)
                else:
                    st8["x_free"] = (sV, sV.n)
                    st8["wbuf_free"] = (sT, sT.n)
                st8["lnf_free"] = (sT, sT.n)
                return ev_lnf

            for _s in (s_stg[0], s_stg[1], s_out):
                SY.wait_ge(_s.h, _s.n)

            for oc in range(NIC):
                ffn_oc(oc, None)
            for _s in (s_stg[0], s_stg[1]):
                SY.wait_ge(_s.h, _s.n)
                S.wait_ge(_s.h, _s.n)
                V.wait_ge(_s.h, _s.n)
            for t in range(np_tiles):
                emit_tile(xp_d, t, phase2=False, tap=False)
            for t in range(nt_tiles):
                emit_tile(xo_d, t, phase2=True, tap=(t == 0))

            for _s in dma_sems:
                SY.wait_ge(_s.h, _s.n)

    return nc


def kernel(**inputs):
    x = np.ascontiguousarray(np.asarray(inputs["L0_attn_x"], dtype=np.float32))
    nc = build()
    zeros = np.zeros((2048, D), np.float32)
    common = {
        "wq": np.ascontiguousarray(inputs["L0_attn_wq"]),
        "wk": np.ascontiguousarray(inputs["L0_attn_wk"]),
        "wv": np.ascontiguousarray(inputs["L0_attn_wv"]),
        "wo": np.ascontiguousarray(inputs["L0_attn_wo"]),
        "fq1w": np.ascontiguousarray(inputs["L0_attn_fq1_w"]),
        "fq2w": np.ascontiguousarray(inputs["L0_attn_fq2_w"]),
        "fk1w": np.ascontiguousarray(inputs["L0_attn_fk1_w"]),
        "fk2w": np.ascontiguousarray(inputs["L0_attn_fk2_w"]),
        "fq1b": np.ascontiguousarray(inputs["L0_attn_fq1_b"]),
        "fq2b": np.ascontiguousarray(inputs["L0_attn_fq2_b"]),
        "fk1b": np.ascontiguousarray(inputs["L0_attn_fk1_b"]),
        "fk2b": np.ascontiguousarray(inputs["L0_attn_fk2_b"]),
        "normw": np.ascontiguousarray(inputs["L0_attn_norm_w"]),
        "pw": np.ascontiguousarray(inputs["L0_ffn_pweight"]).reshape(D, D * 4),
        "nw": np.ascontiguousarray(inputs["L0_ffn_nweight"]).reshape(D, D * 4),
        "exps": np.ascontiguousarray(inputs["L0_ffn_exps"]),
        "bexps": np.ascontiguousarray(inputs["L0_ffn_bexps"]),
        "maskw": np.ascontiguousarray(inputs["L0_ffn_mask_weight"]),
        "fscale": np.ascontiguousarray(inputs["L0_ffn_scale"]),
        "bscale": np.ascontiguousarray(inputs["L0_ffn_biasscale"]),
        "pbias": np.ascontiguousarray(inputs["L0_ffn_pbias"]),
        "nbias": np.ascontiguousarray(inputs["L0_ffn_nbias"]),
    }
    in_maps = []
    for c in range(8):
        b, hf = c // 2, c % 2
        m = dict(common)
        m["xo"] = np.ascontiguousarray(x[b, hf * 2048:(hf + 1) * 2048])
        m["xp"] = zeros if hf == 0 else np.ascontiguousarray(x[b, 0:2048])
        in_maps.append(m)
    res = run_bass_kernel_spmd(nc, in_maps, core_ids=list(range(8)))
    out = np.empty((4, 4096, D), np.float32)
    for c in range(8):
        b, hf = c // 2, c % 2
        out[b, hf * 2048:(hf + 1) * 2048] = res.results[c]["out"]
    return out
